# version 01
# baseline (speedup 1.0000x reference)
import sys
sys.path.insert(0, "/opt/trn_rl_repo")
import numpy as np
import ml_dtypes
import concourse.bass as bass
import concourse.mybir as mybir
from contextlib import ExitStack

F32 = mybir.dt.float32
U32 = mybir.dt.uint32
BF16 = mybir.dt.bfloat16
AF = mybir.ActivationFunctionType
ALU = mybir.AluOpType
AX = mybir.AxisListType
BF = ml_dtypes.bfloat16
LN_EPS = 1e-5


class Ctx:
    def __init__(self, nc):
        self.nc = nc
        self.es = ExitStack()

    def sb(self, name, shape, dt):
        return self.es.enter_context(self.nc.sbuf_tensor(name, shape, dt))

    def ps(self, name, shape, dt):
        return self.es.enter_context(self.nc.psum_tensor(name, shape, dt))

    def sem(self, name):
        return self.es.enter_context(self.nc.semaphore(name))


def build_pre(T=4096, C=1024, serialize=False, stages=8):
    nc = bass.Bass("TRN2", target_bir_lowering=False, detect_race_conditions=False)
    ntile = T // 128
    nch = C // 128
    x0 = nc.dram_tensor("x0", [T, C], F32, kind="ExternalInput").ap()
    gwT = nc.dram_tensor("gwT", [C, 8], F32, kind="ExternalInput").ap()
    gbrep = nc.dram_tensor("gbrep", [128, 8], F32, kind="ExternalInput").ap()
    idn = nc.dram_tensor("idn", [128, 128], BF16, kind="ExternalInput").ap()
    xn0fm = nc.dram_tensor("xn0fm", [C, T], BF16, kind="ExternalOutput").ap()
    gout = nc.dram_tensor("gout", [128, 8 * ntile], F32, kind="ExternalOutput").ap()
    dbg_logit = nc.dram_tensor("dbg_logit", [128, 8 * ntile], F32, kind="ExternalOutput").ap()
    dbg_eexp = nc.dram_tensor("dbg_eexp", [128, 8 * ntile], F32, kind="ExternalOutput").ap()
    dbg_m8 = nc.dram_tensor("dbg_m8", [128, 8 * ntile], F32, kind="ExternalOutput").ap()

    ctx = Ctx(nc)
    xbuf = [ctx.sb(f"xbuf{j}", [128, C], F32) for j in range(2)]
    fm = [ctx.sb(f"fm{c}", [128, T], BF16) for c in range(nch)]
    gsb = ctx.sb("gsb", [128, 8 * ntile], F32)
    dlog = ctx.sb("dlog", [128, 8 * ntile], F32)
    dexp = ctx.sb("dexp", [128, 8 * ntile], F32)
    dm8 = ctx.sb("dm8", [128, 8 * ntile], F32)
    sgw = ctx.sb("sgw", [128, 8 * nch], BF16)
    sgb = ctx.sb("sgb", [128, 8], F32)
    sidn = ctx.sb("sidn", [128, 128], BF16)
    sx = ctx.sb("sx", [128, 1], F32)
    sx2 = ctx.sb("sx2", [128, 1], F32)
    nm2 = ctx.sb("nm2", [128, 1], F32)
    sqcol = ctx.sb("sqcol", [128, 1], F32)
    dvar = ctx.sb("dvar", [128, 1], F32)
    sqscr = ctx.sb("sqscr", [128, 1024], F32)
    nepscol = ctx.sb("nepscol", [128, 1], F32)
    hlncol = ctx.sb("hlncol", [128, 1], F32)
    negm = ctx.sb("negm", [128, 1], F32)
    rstd = ctx.sb("rstd", [128, 1], F32)
    lnv = ctx.sb("lnv", [128, 1], F32)
    xnb = [ctx.sb(f"xnb{j}", [128, C], BF16) for j in range(2)]
    logit = ctx.sb("logit", [128, 8], F32)
    eexp = ctx.sb("eexp", [128, 8], F32)
    m8 = ctx.sb("m8", [128, 8], F32)
    ge8 = ctx.sb("ge8", [128, 8], F32)
    negmx = ctx.sb("negmx", [128, 1], F32)
    rsum = ctx.sb("rsum", [128, 1], F32)
    vdum = ctx.sb("vdum", [128, 1], F32)
    ptr_ = [ctx.ps(f"ptr{j}", [128, 1024], BF16) for j in range(2)]
    pg = ctx.ps("pg", [128, 8], F32)

    ld0 = ctx.sem("ld0")
    ld1 = ctx.sem("ld1")
    gp = ctx.sem("gp")
    ldc = ctx.sem("ldc")
    ldg = ctx.sem("ldg")
    vfree = ctx.sem("vfree")
    vs = ctx.sem("vs")
    sc = ctx.sem("sc")
    te = ctx.sem("te")
    st = ctx.sem("st")
    ser = ctx.sem("ser")
    serk = [0]

    def _ser_in(eng):
        if serialize:
            eng.wait_ge(ser, serk[0])

    def _ser_out(eng):
        if serialize:
            eng.nop().then_inc(ser, 1)
            serk[0] += 1

    def ldw(eng, i):
        eng.wait_ge(ld0 if i % 2 == 0 else ld1, 16 * (i // 2 + 1))

    with ctx.es, nc.Block() as block:

        @block.sync
        def _(sync):
            sync.dma_start(out=sidn[:, :], in_=idn).then_inc(ldc, 16)
            sync.dma_start(out=sgb[:, :], in_=gbrep).then_inc(ldc, 16)
            for i in range(ntile):
                if i >= 2:
                    sync.wait_ge(vs, 6 * (i - 2) + 3)
                sync.dma_start(out=xbuf[i % 2][:, :],
                               in_=x0[128 * i:128 * (i + 1), :]).then_inc(ld0 if i % 2 == 0 else ld1, 16)

        @block.gpsimd
        def _(gpsimd):
            gpsimd.memset(nepscol[:, :], float(C * LN_EPS))
            gpsimd.memset(hlncol[:, :], float(0.5 * np.log(C))).then_inc(gp, 1)
            for c in range(nch):
                gpsimd.dma_start(out=sgw[:, 8 * c:8 * (c + 1)],
                                 in_=gwT[128 * c:128 * (c + 1), :]).then_inc(ldg, 16)

        for i in range(ntile):
            xb = xbuf[i % 2]
            xn = xnb[i % 2]
            pt = ptr_[i % 2]

            @block.vector
            def _(vector, i=i, xb=xb, xn=xn):
                _ser_in(vector)
                ldw(vector, i)
                vector.reduce_sum(sx[:, :], xb[:, :], AX.X)
                vector.tensor_mul(sx2[:, :], sx[:, :], sx[:, :])
                vector.tensor_scalar_mul(nm2[:, :], sx2[:, :], 1.0 / C)
                vector.tensor_scalar_mul(negm[:, :], sx[:, :], -1.0 / C).then_inc(vs, 1)
                vector.wait_ge(sc, 3 * i + 1)
                if i >= 1:
                    vector.wait_ge(sc, 3 * (i - 1) + 2)
                vector.tensor_sub(dvar[:, :], sqcol[:, :], nm2[:, :]).then_inc(vs, 1)
                vector.wait_ge(sc, 3 * i + 2)
                if i >= 2:
                    vector.wait_ge(te, 2 * (i - 2) + 1)
                vector.scalar_tensor_tensor(
                    xn[:, :], xb[:, :], negm[:, :], rstd[:, :].broadcast_to([128, 1024]),
                    ALU.add, ALU.mult).then_inc(vs, 1)
                _ser_out(vector)

            @block.scalar
            def _(scalar, i=i, xb=xb):
                _ser_in(scalar)
                ldw(scalar, i)
                scalar.wait_ge(gp, 1)
                if i >= 1:
                    scalar.wait_ge(vs, 6 * (i - 1) + 2)
                if stages >= 2:
                    scalar.activation(sqscr[:, :], xb[:, :], AF.Square,
                                      accum_out=sqcol[:, :])
                    scalar.drain().then_inc(sc, 1)
                    scalar.wait_ge(vs, 6 * i + 2)
                    scalar.activation(lnv[:, :], dvar[:, :], AF.Ln, bias=nepscol[:, :])
                    scalar.drain()
                    scalar.activation(rstd[:, :], lnv[:, :], AF.Exp, scale=-0.5,
                                      bias=hlncol[:, :])
                    scalar.drain().then_inc(sc, 1)
                else:
                    scalar.nop().then_inc(sc, 1)
                    scalar.nop().then_inc(sc, 1)
                _ser_out(scalar)

            @block.tensor
            def _(tensor, i=i, xn=xn, pt=pt):
                _ser_in(tensor)
                tensor.wait_ge(vs, 6 * i + 3)
                tensor.wait_ge(ldc, 16)
                if i >= 2:
                    tensor.wait_ge(vs, 6 * (i - 2) + 4)
                if stages >= 3:
                    for c in range(nch):
                        mm = tensor.transpose(pt[:, 128 * c:128 * (c + 1)],
                                              xn[:, 128 * c:128 * (c + 1)], sidn[:, :])
                        if c == nch - 1:
                            mm.then_inc(te, 1)
                else:
                    tensor.nop().then_inc(te, 1)
                _ser_out(tensor)

            @block.vector
            def _(vector, i=i, pt=pt):
                _ser_in(vector)
                vector.wait_ge(te, 2 * i + 1)
                if stages >= 4:
                    for c in range(nch):
                        vector.tensor_copy(fm[c][:, 128 * i:128 * (i + 1)],
                                           pt[:, 128 * c:128 * (c + 1)])
                    vector.drain().then_inc(vs, 1)
                else:
                    vector.nop().then_inc(vs, 1)
                _ser_out(vector)

            @block.tensor
            def _(tensor, i=i):
                _ser_in(tensor)
                tensor.wait_ge(vs, 6 * i + 4)
                tensor.wait_ge(ldg, 16 * nch)
                if i >= 1:
                    tensor.wait_ge(vs, 6 * (i - 1) + 5)
                if stages >= 5:
                    for c in range(nch):
                        mm = tensor.matmul(pg[:, :], fm[c][:, 128 * i:128 * (i + 1)],
                                           sgw[:, 8 * c:8 * (c + 1)],
                                           start=(c == 0), stop=(c == nch - 1))
                        if c == nch - 1:
                            mm.then_inc(te, 1)
                else:
                    tensor.nop().then_inc(te, 1)
                _ser_out(tensor)

            @block.vector
            def _(vector, i=i):
                _ser_in(vector)
                vector.wait_ge(te, 2 * i + 2)
                if stages >= 6:
                    if i == 0:
                        vector.wait_ge(ldc, 32)
                    vector.tensor_add(logit[:, :], pg[:, :], sgb[:, :])
                    vector.drain()
                    vector.reduce_max(negmx[:, :], logit[:, :], AX.X)
                    vector.drain()
                    vector.tensor_scalar_mul(negmx[:, :], negmx[:, :], -1.0)
                    vector.drain().then_inc(vs, 1)
                else:
                    vector.nop().then_inc(vs, 1)
                _ser_out(vector)

            @block.scalar
            def _(scalar, i=i):
                _ser_in(scalar)
                scalar.wait_ge(vs, 6 * i + 5)
                if stages >= 7:
                    scalar.activation(eexp[:, :], logit[:, :], AF.Exp,
                                      bias=negmx[:, :])
                    scalar.drain().then_inc(sc, 1)
                else:
                    scalar.nop().then_inc(sc, 1)
                _ser_out(scalar)

            @block.vector
            def _(vector, i=i):
                _ser_in(vector)
                vector.wait_ge(sc, 3 * i + 3)
                if stages >= 8:
                    gcol = gsb[:, 8 * i:8 * (i + 1)]
                    vector.reduce_sum(rsum[:, :], eexp[:, :], AX.X)
                    vector.drain()
                    vector.reciprocal(rsum[:, :], rsum[:, :])
                    vector.drain()
                    vector.tensor_scalar_mul(eexp[:, :], eexp[:, :], rsum[:, :])
                    vector.drain()
                    vector.max(out=m8[:, :], in_=eexp[:, :])
                    vector.drain()
                    vector.tensor_single_scalar(ge8[:, :], eexp[:, :], m8[:, 1:2], ALU.is_ge)
                    vector.drain()
                    vector.tensor_mul(gcol, eexp[:, :], ge8[:, :])
                    vector.drain().then_inc(vs, 1)
                else:
                    vector.nop().then_inc(vs, 1)
                _ser_out(vector)

        @block.sync
        def _(sync):
            sync.wait_ge(vs, 6 * ntile)
            for c in range(nch):
                sync.dma_start(out=xn0fm[128 * c:128 * (c + 1), :], in_=fm[c][:, :]).then_inc(st, 16)
            sync.dma_start(out=gout, in_=gsb[:, :]).then_inc(st, 16)
            sync.dma_start(out=dbg_logit, in_=dlog[:, :]).then_inc(st, 16)
            sync.dma_start(out=dbg_eexp, in_=dexp[:, :]).then_inc(st, 16)
            sync.dma_start(out=dbg_m8, in_=dm8[:, :]).then_inc(st, 16)
            sync.wait_ge(st, 16 * (nch + 4))

    return nc


def pre_inputs(x0b, gate_w, gate_b):
    return dict(
        x0=np.ascontiguousarray(x0b),
        gwT=np.ascontiguousarray(gate_w.T),
        gbrep=np.ascontiguousarray(np.broadcast_to(gate_b, (128, 8))),
        idn=np.eye(128, dtype=BF),
    )


def pre_check(res, x0b, gate_w, gate_b, T=4096):
    m = x0b.mean(axis=-1, keepdims=True)
    v = x0b.var(axis=-1, keepdims=True)
    xn = (x0b - m) / np.sqrt(v + LN_EPS)
    got = np.asarray(res["xn0fm"], dtype=np.float32)
    e1 = np.abs(got - xn.T).max()
    logits = xn @ gate_w.T + gate_b
    lm = logits.max(-1, keepdims=True)
    ee = np.exp(logits - lm)
    p = ee / ee.sum(-1, keepdims=True)
    m2 = np.sort(p, axis=-1)[:, -2:-1]
    gwant = np.where(p >= m2, p, 0.0)
    ntile = T // 128
    ggot = np.asarray(res["gout"]).reshape(128, ntile, 8).transpose(1, 0, 2).reshape(T, 8)
    e2 = np.abs(ggot - gwant).max()
    return e1, e2


def build_attn0(T=4096, C=1024):
    nc = bass.Bass("TRN2", target_bir_lowering=False, detect_race_conditions=False)
    NB = T // 512
    nch = C // 128
    xn0fm = nc.dram_tensor("xn0fm", [C, T], BF16, kind="ExternalInput").ap()
    x0 = nc.dram_tensor("x0", [T, C], F32, kind="ExternalInput").ap()
    Wq = nc.dram_tensor("Wq", [C, C], F32, kind="ExternalInput").ap()
    Wk = nc.dram_tensor("Wk", [C, C], F32, kind="ExternalInput").ap()
    Wv = nc.dram_tensor("Wv", [C, C], F32, kind="ExternalInput").ap()
    bqs = nc.dram_tensor("bqs", [128, nch], F32, kind="ExternalInput").ap()
    bks = nc.dram_tensor("bks", [128, nch], F32, kind="ExternalInput").ap()
    bvrow = nc.dram_tensor("bvrow", [1, C], BF16, kind="ExternalInput").ap()
    mfar = nc.dram_tensor("mfar", [128, 128], F32, kind="ExternalInput").ap()
    mdiag = nc.dram_tensor("mdiag", [128, 128], F32, kind="ExternalInput").ap()
    lnw_bc = nc.dram_tensor("lnw_bc", [128, C], F32, kind="ExternalInput").ap()
    lnb_bc = nc.dram_tensor("lnb_bc", [128, C], F32, kind="ExternalInput").ap()
    acc0 = nc.dram_tensor("acc0", [T, C], F32, kind="ExternalOutput").ap()

    ctx = Ctx(nc)
    sxn = [ctx.sb(f"sxn{c}", [128, T], BF16) for c in range(nch)]
    swq_t = [ctx.sb(f"swqt{c}", [128, C], BF16) for c in range(nch)]
    swk_t = [ctx.sb(f"swkt{c}", [128, C], BF16) for c in range(nch)]
    swv_t = [ctx.sb(f"swvt{c}", [128, C], BF16) for c in range(nch)]
    sbq = ctx.sb("sbq", [128, nch], F32)
    sbk = ctx.sb("sbk", [128, nch], F32)
    sbv = ctx.sb("sbv", [1, C], BF16)
    sones = ctx.sb("sones", [1, 128], BF16)
    smfar = ctx.sb("smfar", [128, 128], F32)
    smdiag = ctx.sb("smdiag", [128, 128], F32)
    slnw = ctx.sb("slnw", [128, C], F32)
    slnb = ctx.sb("slnb", [128, C], F32)
    qb = [[ctx.sb(f"qb{j}_{c}", [128, 512], BF16) for c in range(nch)] for j in range(2)]
    kb = [[ctx.sb(f"kb{j}_{c}", [128, 512], BF16) for c in range(nch)] for j in range(2)]
    vb = [[ctx.sb(f"vb{j}_{t}", [128, C], BF16) for t in range(4)] for j in range(2)]
    sT = [ctx.sb(f"sT{t}", [128, 384], BF16) for t in range(2)]
    araw = ctx.sb("araw", [128, C], F32)
    x0b = [ctx.sb(f"x0b{j}", [128, C], F32) for j in range(2)]
    accb = [ctx.sb(f"accb{j}", [128, C], F32) for j in range(2)]
    sqscr = ctx.sb("sqscr2", [128, C], F32)
    sx = ctx.sb("sx_", [128, 1], F32)
    sx2 = ctx.sb("sx2_", [128, 1], F32)
    nm2 = ctx.sb("nm2_", [128, 1], F32)
    sqcol = ctx.sb("sqcol_", [128, 1], F32)
    dvar = ctx.sb("dvar_", [128, 1], F32)
    negm = ctx.sb("negm_", [128, 1], F32)
    rstd = ctx.sb("rstd_", [128, 1], F32)
    lnv = ctx.sb("lnv_", [128, 1], F32)
    nepscol = ctx.sb("nepscol_", [128, 1], F32)
    hlncol = ctx.sb("hlncol_", [128, 1], F32)
    pqkv = [ctx.ps(f"pqkv{j}", [128, 512], F32) for j in range(2)]
    psT = [ctx.ps(f"psT{j}", [128, 384], F32) for j in range(2)]
    pout = ctx.ps("pout", [128, 1024], F32)

    ld = [ctx.sem(f"ld{j}") for j in range(2)]
    ldc = ctx.sem("ldc")
    ldg = ctx.sem("ldg")
    gp = ctx.sem("gp")
    vs = ctx.sem("vs")
    sc = ctx.sem("sc")
    te = ctx.sem("te")
    st = [ctx.sem(f"st{j}") for j in range(2)]
    cnt = {"vs": 0, "sc": 0, "te": 0, "ldc": 0, "ldg": 0, "ld0": 0, "ld1": 0,
           "st0": 0, "st1": 0}
    ev = {}

    def inc(name, k=1):
        cnt[name] += k
        return cnt[name]

    with ctx.es, nc.Block() as block:

        @block.sync
        def _(sync):
            for ap_, t_ in ((mfar, smfar), (mdiag, smdiag), (lnw_bc, slnw), (lnb_bc, slnb),
                            (bqs, sbq), (bks, sbk)):
                sync.dma_start(out=t_[:, :], in_=ap_).then_inc(ldc, 16)
                inc("ldc")
            for c in range(nch):
                sync.dma_start(out=sxn[c][:, :], in_=xn0fm[128 * c:128 * (c + 1), :]).then_inc(ldc, 16)
                inc("ldc")
            ev["consts"] = cnt["ldc"]
            for it in (0, 1):
                sync.dma_start(out=x0b[it][:, :], in_=x0[128 * it:128 * (it + 1), :]).then_inc(ld[it], 16)
                inc(f"ld{it}")

        @block.gpsimd
        def _(gpsimd):
            gpsimd.memset(nepscol[:, :], float(C * LN_EPS))
            gpsimd.memset(hlncol[:, :], float(0.5 * np.log(C)))
            gpsimd.memset(sones[:, :], 1.0).then_inc(gp, 1)
            gpsimd.dma_start(out=sbv[:, :], in_=bvrow).then_inc(ldg, 16)
            inc("ldg")
            for c in range(nch):
                gpsimd.dma_start(out=swq_t[c][:, :], in_=Wq[128 * c:128 * (c + 1), :]).then_inc(ldg, 16)
                inc("ldg")
                gpsimd.dma_start(out=swk_t[c][:, :], in_=Wk[128 * c:128 * (c + 1), :]).then_inc(ldg, 16)
                inc("ldg")
                gpsimd.dma_start(out=swv_t[c][:, :], in_=Wv[128 * c:128 * (c + 1), :]).then_inc(ldg, 16)
                inc("ldg")
            ev["weights"] = cnt["ldg"]

        for b in range(NB):
            j = b % 2
            t0 = 512 * b

            for m in range(nch):
                pj = m % 2

                @block.tensor
                def _(tensor, b=b, m=m, pj=pj, j=j, t0=t0):
                    if b == 0 and m == 0:
                        tensor.wait_ge(ldg, 16 * ev["weights"])
                        tensor.wait_ge(ldc, 16 * ev["consts"])
                        tensor.wait_ge(gp, 1)
                    key = (f"qe_{b}_{m - 2}" if m >= 2 else
                           (f"ve_{b - 1}_{6 + m}" if b >= 1 else None))
                    if key:
                        tensor.wait_ge(sc if key.startswith("qe") else vs, ev[key])
                    for c in range(nch):
                        mm = tensor.matmul(pqkv[pj][:, :], swq_t[c][:, 128 * m:128 * (m + 1)],
                                           sxn[c][:, t0:t0 + 512],
                                           start=(c == 0), stop=(c == nch - 1))
                    mm.then_inc(te, 1)
                    ev[f"qmm_{b}_{m}"] = inc("te")

                @block.scalar
                def _(scalar, b=b, m=m, pj=pj, j=j):
                    scalar.wait_ge(te, ev[f"qmm_{b}_{m}"])
                    if b >= 2 and m == 0:
                        scalar.wait_ge(te, ev[f"sTmm_{b - 2}_3"])
                    scalar.activation(qb[j][m][:, :], pqkv[pj][:, :], AF.Identity,
                                      bias=sbq[:, m:m + 1], scale=1.0 / 512.0)
                    scalar.drain().then_inc(sc, 1)
                    ev[f"qe_{b}_{m}"] = inc("sc")

            for m in range(nch):
                pj = m % 2

                @block.tensor
                def _(tensor, b=b, m=m, pj=pj, j=j, t0=t0):
                    key = f"ke_{b}_{m - 2}" if m >= 2 else f"qe_{b}_{nch - 2 + m}"
                    tensor.wait_ge(sc, ev[key])
                    for c in range(nch):
                        mm = tensor.matmul(pqkv[pj][:, :], swk_t[c][:, 128 * m:128 * (m + 1)],
                                           sxn[c][:, t0:t0 + 512],
                                           start=(c == 0), stop=(c == nch - 1))
                    mm.then_inc(te, 1)
                    ev[f"kmm_{b}_{m}"] = inc("te")

                @block.scalar
                def _(scalar, b=b, m=m, pj=pj, j=j):
                    scalar.wait_ge(te, ev[f"kmm_{b}_{m}"])
                    scalar.activation(kb[j][m][:, :], pqkv[pj][:, :], AF.Identity,
                                      bias=sbk[:, m:m + 1])
                    scalar.drain().then_inc(sc, 1)
                    ev[f"ke_{b}_{m}"] = inc("sc")

            for r in range(8):
                tt, h = r // 2, r % 2
                pj = r % 2

                @block.tensor
                def _(tensor, b=b, r=r, tt=tt, h=h, pj=pj, j=j, t0=t0):
                    key = f"ve_{b}_{r - 2}" if r >= 2 else f"ke_{b}_{nch - 2 + r}"
                    tensor.wait_ge(vs if key.startswith("ve") else sc, ev[key])
                    for c in range(nch):
                        tensor.matmul(pqkv[pj][:, :], sxn[c][:, t0 + 128 * tt:t0 + 128 * (tt + 1)],
                                      swv_t[c][:, 512 * h:512 * (h + 1)],
                                      start=(c == 0), stop=False)
                    mm = tensor.matmul(pqkv[pj][:, :], sones[:, :], sbv[:, 512 * h:512 * (h + 1)],
                                       start=False, stop=True)
                    mm.then_inc(te, 1)
                    ev[f"vmm_{b}_{r}"] = inc("te")

                @block.vector
                def _(vector, b=b, r=r, tt=tt, h=h, pj=pj, j=j):
                    vector.wait_ge(te, ev[f"vmm_{b}_{r}"])
                    if b >= 2 and r == 0:
                        vector.wait_ge(te, ev[f"omm_{b - 2}_3"])
                    vector.tensor_copy(vb[j][tt][:, 512 * h:512 * (h + 1)], pqkv[pj][:, :])
                    vector.drain().then_inc(vs, 1)
                    ev[f"ve_{b}_{r}"] = inc("vs")

            for tt in range(4):
                it = 4 * b + tt
                r0 = 128 * it
                chunks = [cc for cc in (r0 - 256, r0 - 128, r0) if cc >= 0]
                stj = sT[tt % 2]
                pstj = psT[tt % 2]
                jj = it % 2

                @block.tensor
                def _(tensor, b=b, tt=tt, chunks=chunks, stj=stj, pstj=pstj, j=j, t0=t0):
                    tensor.wait_ge(vs, ev[f"ve_{b}_7"])
                    key = f"sTe_{b}_{tt - 2}" if tt >= 2 else (f"sTe_{b - 1}_{tt + 2}" if b >= 1 else None)
                    if key:
                        tensor.wait_ge(vs, ev[key])
                    for ci, cc in enumerate(chunks):
                        kblk = j if cc >= t0 else 1 - j
                        koff = cc - t0 if cc >= t0 else cc - (t0 - 512)
                        for c in range(nch):
                            mm = tensor.matmul(pstj[:, 128 * ci:128 * (ci + 1)],
                                               kb[kblk][c][:, koff:koff + 128],
                                               qb[j][c][:, 128 * tt:128 * (tt + 1)],
                                               start=(c == 0), stop=(c == nch - 1))
                    mm.then_inc(te, 1)
                    ev[f"sTmm_{b}_{tt}"] = inc("te")

                @block.vector
                def _(vector, b=b, tt=tt, chunks=chunks, stj=stj, pstj=pstj):
                    vector.wait_ge(te, ev[f"sTmm_{b}_{tt}"])
                    nchk = len(chunks)
                    for ci in range(nchk):
                        if ci == nchk - 1:
                            vector.tensor_mul(stj[:, 128 * ci:128 * (ci + 1)],
                                              pstj[:, 128 * ci:128 * (ci + 1)], smdiag[:, :])
                        elif ci == 0 and nchk == 3:
                            vector.tensor_mul(stj[:, 128 * ci:128 * (ci + 1)],
                                              pstj[:, 128 * ci:128 * (ci + 1)], smfar[:, :])
                        else:
                            vector.tensor_copy(stj[:, 128 * ci:128 * (ci + 1)],
                                               pstj[:, 128 * ci:128 * (ci + 1)])
                    vector.drain().then_inc(vs, 1)
                    ev[f"sTe_{b}_{tt}"] = inc("vs")

                @block.tensor
                def _(tensor, b=b, tt=tt, chunks=chunks, stj=stj, j=j, t0=t0):
                    tensor.wait_ge(vs, ev[f"sTe_{b}_{tt}"])
                    key = f"oc_{b}_{tt - 1}" if tt >= 1 else (f"oc_{b - 1}_3" if b >= 1 else None)
                    if key:
                        tensor.wait_ge(vs, ev[key])
                    for h in range(2):
                        for ci, cc in enumerate(chunks):
                            vblk = j if cc >= t0 else 1 - j
                            voff = cc - t0 if cc >= t0 else cc - (t0 - 512)
                            vtile = voff // 128
                            mm = tensor.matmul(pout[:, 512 * h:512 * (h + 1)],
                                               stj[:, 128 * ci:128 * (ci + 1)],
                                               vb[vblk][vtile][:, 512 * h:512 * (h + 1)],
                                               start=(ci == 0), stop=(ci == len(chunks) - 1))
                    mm.then_inc(te, 1)
                    ev[f"omm_{b}_{tt}"] = inc("te")

                @block.vector
                def _(vector, b=b, tt=tt, it=it):
                    vector.wait_ge(te, ev[f"omm_{b}_{tt}"])
                    if it >= 1:
                        vector.wait_ge(sc, ev[f"osq_{it - 1}"])
                    vector.tensor_copy(araw[:, 0:512], pout[:, 0:512])
                    vector.tensor_copy(araw[:, 512:1024], pout[:, 512:1024])
                    vector.drain().then_inc(vs, 1)
                    ev[f"oc_{b}_{tt}"] = inc("vs")
                    vector.reduce_sum(sx[:, :], araw[:, :], AX.X)
                    vector.drain()
                    vector.tensor_mul(sx2[:, :], sx[:, :], sx[:, :])
                    vector.tensor_scalar_mul(negm[:, :], sx[:, :], -1.0 / C)
                    vector.drain()
                    vector.tensor_scalar_mul(nm2[:, :], sx2[:, :], 1.0 / C)
                    vector.drain().then_inc(vs, 1)
                    ev[f"ost_{it}"] = inc("vs")

                @block.scalar
                def _(scalar, b=b, tt=tt, it=it):
                    scalar.wait_ge(vs, ev[f"ost_{it}"])
                    scalar.activation(sqscr[:, :], araw[:, :], AF.Square, accum_out=sqcol[:, :])
                    scalar.drain().then_inc(sc, 1)
                    ev[f"osq_{it}"] = inc("sc")

                @block.vector
                def _(vector, b=b, tt=tt, it=it):
                    vector.wait_ge(sc, ev[f"osq_{it}"])
                    vector.tensor_sub(dvar[:, :], sqcol[:, :], nm2[:, :])
                    vector.drain().then_inc(vs, 1)
                    ev[f"odv_{it}"] = inc("vs")

                @block.scalar
                def _(scalar, b=b, tt=tt, it=it):
                    scalar.wait_ge(vs, ev[f"odv_{it}"])
                    scalar.activation(lnv[:, :], dvar[:, :], AF.Ln, bias=nepscol[:, :])
                    scalar.drain()
                    scalar.activation(rstd[:, :], lnv[:, :], AF.Exp, scale=-0.5, bias=hlncol[:, :])
                    scalar.drain().then_inc(sc, 1)
                    ev[f"ors_{it}"] = inc("sc")

                @block.vector
                def _(vector, b=b, tt=tt, it=it, jj=jj):
                    vector.wait_ge(sc, ev[f"ors_{it}"])
                    vector.wait_ge(ld[jj], 16 * (it // 2 + 1))
                    if it >= 2:
                        vector.wait_ge(st[jj], 16 * (it // 2))
                    vector.scalar_tensor_tensor(araw[:, :], araw[:, :], negm[:, :],
                                                rstd[:, :].broadcast_to([128, C]),
                                                ALU.add, ALU.mult)
                    vector.drain()
                    vector.tensor_mul(araw[:, :], araw[:, :], slnw[:, :])
                    vector.tensor_add(araw[:, :], araw[:, :], slnb[:, :])
                    vector.tensor_add(accb[jj][:, :], araw[:, :], x0b[jj][:, :])
                    vector.drain().then_inc(vs, 1)
                    ev[f"acc_{it}"] = inc("vs")

                @block.sync
                def _(sync, b=b, tt=tt, it=it, jj=jj):
                    sync.wait_ge(vs, ev[f"acc_{it}"])
                    sync.dma_start(out=acc0[128 * it:128 * (it + 1), :],
                                   in_=accb[jj][:, :]).then_inc(st[jj], 16)
                    inc(f"st{jj}")
                    nxt = it + 2
                    if nxt < T // 128:
                        sync.dma_start(out=x0b[jj][:, :],
                                       in_=x0[128 * nxt:128 * (nxt + 1), :]).then_inc(ld[jj], 16)
                        inc(f"ld{jj}")

        @block.sync
        def _(sync):
            sync.wait_ge(st[0], 16 * cnt["st0"])
            sync.wait_ge(st[1], 16 * cnt["st1"])

    return nc


def attn0_inputs(xn0fm, x0b, W_qkv, b_qkv, ln_w, ln_b):
    C = 1024
    mfar = np.tril(np.ones((128, 128), dtype=np.float32), -1)
    mdiag = np.triu(np.ones((128, 128), dtype=np.float32), 0)
    def colmaj(v):
        return np.ascontiguousarray(v.reshape(-1, 128).T)
    return dict(
        xn0fm=xn0fm, x0=np.ascontiguousarray(x0b),
        Wq=np.ascontiguousarray(W_qkv[:, 0:C]),
        Wk=np.ascontiguousarray(W_qkv[:, C:2 * C]),
        Wv=np.ascontiguousarray(W_qkv[:, 2 * C:3 * C]),
        bqs=colmaj(b_qkv[0:C] / 512.0),
        bks=colmaj(b_qkv[C:2 * C]),
        bvrow=np.ascontiguousarray(b_qkv[2 * C:3 * C].astype(BF).reshape(1, C)),
        mfar=mfar, mdiag=mdiag,
        lnw_bc=np.ascontiguousarray(np.broadcast_to(ln_w, (128, C))),
        lnb_bc=np.ascontiguousarray(np.broadcast_to(ln_b, (128, C))),
    )


def build_moe(T=4096, C=1024, H=4096, E=8):
    nc = bass.Bass("TRN2", target_bir_lowering=False, detect_race_conditions=False)
    nch = C // 128
    HH = H // 2
    nhh = HH // 128
    NTB = T // 512
    xn0fm = nc.dram_tensor("xn0fm", [C, T], BF16, kind="ExternalInput").ap()
    acc0 = nc.dram_tensor("acc0", [T, C], F32, kind="ExternalInput").ap()
    w1 = nc.dram_tensor("w1", [E, C, H], F32, kind="ExternalInput").ap()
    w2 = nc.dram_tensor("w2", [E, H, C], F32, kind="ExternalInput").ap()
    b1cm = nc.dram_tensor("b1cm", [E, 128, H // 128], F32, kind="ExternalInput").ap()
    b2bf = nc.dram_tensor("b2bf", [1, E * C], BF16, kind="ExternalInput").ap()
    gin = nc.dram_tensor("gin", [128, 8 * (T // 128)], F32, kind="ExternalInput").ap()
    acc1 = nc.dram_tensor("acc1", [T, C], F32, kind="ExternalOutput").ap()

    ctx = Ctx(nc)
    sw1 = [[ctx.sb(f"sw1_{j}_{c}", [128, HH], BF16) for c in range(nch)] for j in range(2)]
    sw2 = [[ctx.sb(f"sw2_{j}_{m}", [128, C], BF16) for m in range(nhh)] for j in range(2)]
    sxnb = [[ctx.sb(f"sxnb{j}_{c}", [128, 512], BF16) for c in range(nch)] for j in range(2)]
    sb1 = [ctx.sb(f"sb1_{j}", [128, nhh], F32) for j in range(2)]
    sb2 = ctx.sb("sb2", [1, E * C], BF16)
    sg = ctx.sb("sg", [128, 8 * (T // 128)], F32)
    sones = ctx.sb("sones", [1, 128], BF16)
    hbuf = [ctx.sb(f"hbuf{m}", [128, 512], BF16) for m in range(nhh)]
    ystg = [ctx.sb(f"ystg{j}", [128, 1024], F32) for j in range(2)]
    ph = [ctx.ps(f"ph{j}", [128, 512], F32) for j in range(2)]
    py = [ctx.ps(f"py{j}", [128, 512], F32) for j in range(2)]

    ldc = ctx.sem("ldc")
    ldg = ctx.sem("ldg")
    ldx = [ctx.sem(f"ldx{j}") for j in range(2)]
    gp = ctx.sem("gp")
    vs = ctx.sem("vs")
    sc = ctx.sem("sc")
    te = ctx.sem("te")
    sta = ctx.sem("sta")
    cnt = {"ldc": 0, "ldg": 0, "ldx0": 0, "ldx1": 0, "vs": 0, "sc": 0, "te": 0, "sta": 0}
    ev = {}

    def inc(name, k=1):
        cnt[name] += k
        return cnt[name]

    nunit = 2 * E
    NGI = nunit * NTB

    with ctx.es, nc.Block() as block:

        @block.sync
        def _(sync):
            sync.dma_start(out=sg[:, :], in_=gin).then_inc(ldc, 16)
            inc("ldc")
            sync.dma_start(out=acc1, in_=acc0).then_inc(ldc, 16)
            inc("ldc")
            ev["seed"] = cnt["ldc"]
            for gi0 in (0, 1):
                tb0 = gi0 % NTB
                for c in range(nch):
                    sync.dma_start(out=sxnb[gi0][c][:, :],
                                   in_=xn0fm[128 * c:128 * (c + 1), 512 * tb0:512 * (tb0 + 1)]).then_inc(ldx[gi0], 16)
                    inc(f"ldx{gi0}")
                ev[f"xnld_{gi0}"] = cnt[f"ldx{gi0}"]

        @block.gpsimd
        def _(gpsimd):
            gpsimd.memset(sones[:, :], 1.0).then_inc(gp, 1)
            gpsimd.dma_start(out=sb2[:, :], in_=b2bf).then_inc(ldg, 16)
            inc("ldg")

        for u in range(nunit):
            e, h = u // 2, u % 2
            wj = u % 2

            @block.gpsimd
            def _(gpsimd, u=u, e=e, h=h, wj=wj):
                if u >= 2:
                    gpsimd.wait_ge(te, ev[f"unit_mm_done_{u - 2}"])
                for c in range(nch):
                    gpsimd.dma_start(out=sw1[wj][c][:, :],
                                     in_=w1[e, 128 * c:128 * (c + 1), HH * h:HH * (h + 1)]).then_inc(ldg, 16)
                    inc("ldg")
                for m in range(nhh):
                    gpsimd.dma_start(out=sw2[wj][m][:, :],
                                     in_=w2[e, HH * h + 128 * m:HH * h + 128 * (m + 1), :]).then_inc(ldg, 16)
                    inc("ldg")
                gpsimd.dma_start(out=sb1[wj][:, :],
                                 in_=b1cm[e, :, nhh * h:nhh * (h + 1)]).then_inc(ldg, 16)
                inc("ldg")
                ev[f"wload_{u}"] = cnt["ldg"]

            for tb in range(NTB):
                gi = u * NTB + tb
                xj = gi % 2
                t0 = 512 * tb

                for m in range(nhh):
                    pj = m % 2

                    @block.tensor
                    def _(tensor, u=u, tb=tb, gi=gi, m=m, pj=pj, wj=wj, xj=xj):
                        if m == 0:
                            if tb == 0:
                                tensor.wait_ge(ldg, 16 * ev[f"wload_{u}"])
                            tensor.wait_ge(ldx[xj], 16 * ev[f"xnld_{gi}"])
                            if gi >= 1:
                                tensor.wait_ge(te, ev[f"ymm_done_{gi - 1}"])
                        key = f"he_{gi}_{m - 2}"
                        if key in ev:
                            tensor.wait_ge(sc, ev[key])
                        elif gi >= 1:
                            tensor.wait_ge(sc, ev[f"he_{gi - 1}_{nhh - 2 + m}"])
                        for c in range(nch):
                            mm = tensor.matmul(ph[pj][:, :], sw1[wj][c][:, 128 * m:128 * (m + 1)],
                                               sxnb[xj][c][:, :],
                                               start=(c == 0), stop=(c == nch - 1))
                        mm.then_inc(te, 1)
                        ev[f"hmm_{gi}_{m}"] = inc("te")

                    @block.scalar
                    def _(scalar, gi=gi, m=m, pj=pj, wj=wj):
                        scalar.wait_ge(te, ev[f"hmm_{gi}_{m}"])
                        scalar.activation(hbuf[m][:, :], ph[pj][:, :], AF.Relu,
                                          bias=sb1[wj][:, m:m + 1])
                        scalar.drain().then_inc(sc, 1)
                        ev[f"he_{gi}_{m}"] = inc("sc")

                @block.sync
                def _(sync, gi=gi, xj=xj):
                    ngi = gi + 2
                    if ngi >= NGI:
                        return
                    nu, ntb = ngi // NTB, ngi % NTB
                    sync.wait_ge(te, ev[f"hmm_{gi}_{nhh - 1}"])
                    for c in range(nch):
                        sync.dma_start(out=sxnb[xj][c][:, :],
                                       in_=xn0fm[128 * c:128 * (c + 1), 512 * ntb:512 * (ntb + 1)]).then_inc(ldx[xj], 16)
                        inc(f"ldx{xj}")
                    ev[f"xnld_{ngi}"] = cnt[f"ldx{xj}"]

                for r in range(8):
                    tt, oh = r // 2, r % 2
                    pj = r % 2

                    @block.tensor
                    def _(tensor, u=u, e=e, h=h, tb=tb, gi=gi, r=r, tt=tt, oh=oh, pj=pj, wj=wj):
                        tensor.wait_ge(sc, ev[f"he_{gi}_{nhh - 1}"])
                        key = f"ye_{gi}_{r - 2}"
                        if key in ev:
                            tensor.wait_ge(vs, ev[key])
                        elif gi >= 1:
                            tensor.wait_ge(vs, ev[f"ye_{gi - 1}_{6 + r}"])
                        if r == 0 and u == 0 and tb == 0:
                            tensor.wait_ge(gp, 1)
                            tensor.wait_ge(ldg, 16)
                        last_stop = (h != 0)
                        for m in range(nhh):
                            mm = tensor.matmul(py[pj][:, :], hbuf[m][:, 128 * tt:128 * (tt + 1)],
                                               sw2[wj][m][:, 512 * oh:512 * (oh + 1)],
                                               start=(m == 0),
                                               stop=(last_stop and m == nhh - 1))
                        if h == 0:
                            mm = tensor.matmul(py[pj][:, :], sones[:, :],
                                               sb2[:, C * e + 512 * oh:C * e + 512 * (oh + 1)],
                                               start=False, stop=True)
                        mm.then_inc(te, 1)
                        ev[f"ymm_{gi}_{r}"] = inc("te")

                    @block.vector
                    def _(vector, e=e, tb=tb, gi=gi, r=r, tt=tt, oh=oh, pj=pj, xj=xj):
                        vector.wait_ge(te, ev[f"ymm_{gi}_{r}"])
                        si = gi * 4 + tt
                        if oh == 0 and si >= 2:
                            vector.wait_ge(sta, 16 * ev[f"accst_{si - 2}"])
                        it = 4 * tb + tt
                        gcol = sg[:, 8 * it + e:8 * it + e + 1]
                        if r == 0 and gi == 0:
                            vector.wait_ge(ldc, 16)
                        vector.tensor_scalar(ystg[si % 2][:, 512 * oh:512 * (oh + 1)],
                                             py[pj][:, :], gcol, None, ALU.mult)
                        vector.drain().then_inc(vs, 1)
                        ev[f"ye_{gi}_{r}"] = inc("vs")

                    if oh == 1:
                        @block.gpsimd
                        def _(gpsimd, gi=gi, tt=tt, t0=t0):
                            si = gi * 4 + tt
                            gpsimd.wait_ge(vs, ev[f"ye_{gi}_{2 * tt + 1}"])
                            if si == 0:
                                gpsimd.wait_ge(ldc, 16 * ev["seed"])
                            gpsimd.dma_start(out=acc1[t0 + 128 * tt:t0 + 128 * (tt + 1), :],
                                             in_=ystg[si % 2][:, :], accum_op=ALU.add).then_inc(sta, 16)
                            ev[f"accst_{si}"] = inc("sta")

                ev[f"ymm_done_{gi}"] = ev[f"ymm_{gi}_7"]

            ev[f"unit_mm_done_{u}"] = ev[f"ymm_{u * NTB + NTB - 1}_7"]

        @block.gpsimd
        def _(gpsimd):
            gpsimd.wait_ge(sta, 16 * cnt["sta"])

    return nc


def moe_inputs(xn0fm, acc0, w1, w2, b1, b2, gin):
    E, C, H = 8, 1024, 4096
    b1cm = np.stack([np.ascontiguousarray(b1[e].reshape(H // 128, 128).T) for e in range(E)])
    return dict(
        xn0fm=xn0fm, acc0=np.ascontiguousarray(acc0),
        w1=np.ascontiguousarray(w1), w2=np.ascontiguousarray(w2),
        b1cm=np.ascontiguousarray(b1cm),
        b2bf=np.ascontiguousarray(b2.astype(BF).reshape(1, E * C)),
        gin=gin,
    )


def build_ln1(T=4096, C=1024):
    nc = bass.Bass("TRN2", target_bir_lowering=False, detect_race_conditions=False)
    ntile = T // 128
    nch = C // 128
    x1 = nc.dram_tensor("x1", [T, C], F32, kind="ExternalInput").ap()
    g1bc = nc.dram_tensor("g1bc", [128, C], F32, kind="ExternalInput").ap()
    b1bc = nc.dram_tensor("b1bc", [128, C], F32, kind="ExternalInput").ap()
    idn = nc.dram_tensor("idn", [128, 128], BF16, kind="ExternalInput").ap()
    idnf = nc.dram_tensor("idnf", [128, 128], F32, kind="ExternalInput").ap()
    xn1f = nc.dram_tensor("xn1f", [C, T], F32, kind="ExternalOutput").ap()
    xn1h = nc.dram_tensor("xn1h", [C, T], BF16, kind="ExternalOutput").ap()
    l1x = nc.dram_tensor("l1x", [C, T], BF16, kind="ExternalOutput").ap()

    ctx = Ctx(nc)
    xbuf = [ctx.sb(f"xbuf{j}", [128, C], F32) for j in range(2)]
    xnt = [ctx.sb(f"xnt{j}", [128, C], F32) for j in range(2)]
    lxt = [ctx.sb(f"lxt{j}", [128, C], BF16) for j in range(2)]
    fmf = [ctx.sb(f"fmf{c}", [128, T], F32) for c in range(nch)]
    lhbufs = [ctx.sb(f"lhb{j}", [128, C], BF16) for j in range(2)]
    sidn = ctx.sb("sidn", [128, 128], BF16)
    sidnf = ctx.sb("sidnf", [128, 128], F32)
    sg1 = ctx.sb("sg1", [128, C], F32)
    sb1 = ctx.sb("sb1", [128, C], F32)
    sx = ctx.sb("sx", [128, 1], F32)
    sx2 = ctx.sb("sx2", [128, 1], F32)
    nm2 = ctx.sb("nm2", [128, 1], F32)
    sqcol = ctx.sb("sqcol", [128, 1], F32)
    dvar = ctx.sb("dvar", [128, 1], F32)
    sqscr = ctx.sb("sqscr", [128, C], F32)
    nepscol = ctx.sb("nepscol", [128, 1], F32)
    hlncol = ctx.sb("hlncol", [128, 1], F32)
    negm = ctx.sb("negm", [128, 1], F32)
    rstd = ctx.sb("rstd", [128, 1], F32)
    lnv = ctx.sb("lnv", [128, 1], F32)
    ptf = [ctx.ps(f"ptf{j}", [128, 512], F32) for j in range(2)]
    pth = ctx.ps("pth", [128, 1024], BF16)

    ld = [ctx.sem(f"ld{j}") for j in range(2)]
    ldc = ctx.sem("ldc")
    ldg = ctx.sem("ldg")
    gp = ctx.sem("gp")
    vs = ctx.sem("vs")
    sc = ctx.sem("sc")
    te = ctx.sem("te")
    st = ctx.sem("st")
    cnt = {"ld0": 0, "ld1": 0, "ldc": 0, "ldg": 0, "vs": 0, "sc": 0, "te": 0, "st": 0}
    ev = {}

    def inc(n, k=1):
        cnt[n] += k
        return cnt[n]

    with ctx.es, nc.Block() as block:

        @block.sync
        def _(sync):
            for ap_, t_ in ((g1bc, sg1), (b1bc, sb1), (idn, sidn), (idnf, sidnf)):
                sync.dma_start(out=t_[:, :], in_=ap_).then_inc(ldc, 16)
                inc("ldc")
            for i in (0, 1):
                sync.dma_start(out=xbuf[i][:, :], in_=x1[128 * i:128 * (i + 1), :]).then_inc(ld[i], 16)
                inc(f"ld{i}")

        @block.gpsimd
        def _(gpsimd):
            gpsimd.memset(nepscol[:, :], float(C * LN_EPS))
            gpsimd.memset(hlncol[:, :], float(0.5 * np.log(C))).then_inc(gp, 1)

        for i in range(ntile):
            j = i % 2
            xb, xn, lx = xbuf[j], xnt[j], lxt[j]

            @block.vector
            def _(vector, i=i, j=j, xb=xb):
                vector.wait_ge(ld[j], 16 * (i // 2 + 1))
                if i >= 1:
                    vector.wait_ge(sc, ev[f"rs2_{i - 1}"])
                vector.reduce_sum(sx[:, :], xb[:, :], AX.X)
                vector.drain()
                vector.tensor_mul(sx2[:, :], sx[:, :], sx[:, :])
                vector.tensor_scalar_mul(negm[:, :], sx[:, :], -1.0 / C)
                vector.drain()
                vector.tensor_scalar_mul(nm2[:, :], sx2[:, :], 1.0 / C)
                vector.drain().then_inc(vs, 1)
                ev[f"st1_{i}"] = inc("vs")

            @block.scalar
            def _(scalar, i=i, xb=xb):
                scalar.wait_ge(vs, ev[f"st1_{i}"])
                scalar.wait_ge(gp, 1)
                scalar.activation(sqscr[:, :], xb[:, :], AF.Square, accum_out=sqcol[:, :])
                scalar.drain().then_inc(sc, 1)
                ev[f"sq1_{i}"] = inc("sc")

            @block.vector
            def _(vector, i=i):
                vector.wait_ge(sc, ev[f"sq1_{i}"])
                vector.tensor_sub(dvar[:, :], sqcol[:, :], nm2[:, :])
                vector.drain().then_inc(vs, 1)
                ev[f"dv1_{i}"] = inc("vs")

            @block.scalar
            def _(scalar, i=i):
                scalar.wait_ge(vs, ev[f"dv1_{i}"])
                scalar.activation(lnv[:, :], dvar[:, :], AF.Ln, bias=nepscol[:, :])
                scalar.drain()
                scalar.activation(rstd[:, :], lnv[:, :], AF.Exp, scale=-0.5, bias=hlncol[:, :])
                scalar.drain().then_inc(sc, 1)
                ev[f"rs1_{i}"] = inc("sc")

            @block.vector
            def _(vector, i=i, xb=xb, xn=xn):
                vector.wait_ge(sc, ev[f"rs1_{i}"])
                if i >= 2:
                    vector.wait_ge(te, ev[f"trf_{i - 2}"])
                    vector.wait_ge(sc, ev[f"sq2_{i - 2}"])
                vector.scalar_tensor_tensor(xn[:, :], xb[:, :], negm[:, :],
                                            rstd[:, :].broadcast_to([128, C]), ALU.add, ALU.mult)
                vector.drain().then_inc(vs, 1)
                ev[f"xn_{i}"] = inc("vs")
                vector.reduce_sum(sx[:, :], xn[:, :], AX.X)
                vector.drain()
                vector.tensor_mul(sx2[:, :], sx[:, :], sx[:, :])
                vector.tensor_scalar_mul(negm[:, :], sx[:, :], -1.0 / C)
                vector.drain()
                vector.tensor_scalar_mul(nm2[:, :], sx2[:, :], 1.0 / C)
                vector.drain().then_inc(vs, 1)
                ev[f"st2_{i}"] = inc("vs")

            @block.scalar
            def _(scalar, i=i, xn=xn):
                scalar.wait_ge(vs, ev[f"st2_{i}"])
                scalar.activation(sqscr[:, :], xn[:, :], AF.Square, accum_out=sqcol[:, :])
                scalar.drain().then_inc(sc, 1)
                ev[f"sq2_{i}"] = inc("sc")

            @block.vector
            def _(vector, i=i):
                vector.wait_ge(sc, ev[f"sq2_{i}"])
                vector.tensor_sub(dvar[:, :], sqcol[:, :], nm2[:, :])
                vector.drain().then_inc(vs, 1)
                ev[f"dv2_{i}"] = inc("vs")

            @block.scalar
            def _(scalar, i=i):
                scalar.wait_ge(vs, ev[f"dv2_{i}"])
                scalar.activation(lnv[:, :], dvar[:, :], AF.Ln, bias=nepscol[:, :])
                scalar.drain()
                scalar.activation(rstd[:, :], lnv[:, :], AF.Exp, scale=-0.5, bias=hlncol[:, :])
                scalar.drain().then_inc(sc, 1)
                ev[f"rs2_{i}"] = inc("sc")

            @block.vector
            def _(vector, i=i, xn=xn, lx=lx):
                vector.wait_ge(sc, ev[f"rs2_{i}"])
                vector.wait_ge(ldc, 16 * 2)
                if i >= 2:
                    vector.wait_ge(te, ev[f"trh_{i - 2}"])
                vector.scalar_tensor_tensor(sqscr[:, :], xn[:, :], negm[:, :],
                                            rstd[:, :].broadcast_to([128, C]), ALU.add, ALU.mult)
                vector.drain()
                vector.tensor_mul(sqscr[:, :], sqscr[:, :], sg1[:, :])
                vector.drain()
                vector.tensor_add(lx[:, :], sqscr[:, :], sb1[:, :])
                vector.drain().then_inc(vs, 1)
                ev[f"lx_{i}"] = inc("vs")

            @block.tensor
            def _(tensor, i=i, xn=xn):
                tensor.wait_ge(vs, ev[f"xn_{i}"])
                tensor.wait_ge(ldc, 16 * 4)
                if i >= 1:
                    tensor.wait_ge(vs, ev[f"evf_{i - 1}"])
                for c in range(nch):
                    mm = tensor.transpose(ptf[c % 2][:, 128 * (c // 2):128 * (c // 2 + 1)],
                                          xn[:, 128 * c:128 * (c + 1)], sidnf[:, :])
                mm.then_inc(te, 1)
                ev[f"trf_{i}"] = inc("te")

            @block.vector
            def _(vector, i=i):
                vector.wait_ge(te, ev[f"trf_{i}"])
                for c in range(nch):
                    vector.tensor_copy(fmf[c][:, 128 * i:128 * (i + 1)],
                                       ptf[c % 2][:, 128 * (c // 2):128 * (c // 2 + 1)])
                vector.drain().then_inc(vs, 1)
                ev[f"evf_{i}"] = inc("vs")

            @block.tensor
            def _(tensor, i=i, lx=lx):
                tensor.wait_ge(vs, ev[f"lx_{i}"])
                if i >= 1:
                    tensor.wait_ge(vs, ev[f"evh_{i - 1}"])
                for c in range(nch):
                    mm = tensor.transpose(pth[:, 128 * c:128 * (c + 1)], lx[:, 128 * c:128 * (c + 1)], sidn[:, :])
                mm.then_inc(te, 1)
                ev[f"trh_{i}"] = inc("te")

            @block.vector
            def _(vector, i=i):
                vector.wait_ge(te, ev[f"trh_{i}"])
                if i >= 2:
                    vector.wait_ge(st, 16 * ev[f"lst_{i - 2}"])
                for c in range(nch):
                    vector.tensor_copy(lhbufs[i % 2][:, 128 * c:128 * (c + 1)], pth[:, 128 * c:128 * (c + 1)])
                vector.drain().then_inc(vs, 1)
                ev[f"evh_{i}"] = inc("vs")

            @block.sync
            def _(sync, i=i, j=j):
                sync.wait_ge(vs, ev[f"evh_{i}"])
                for c in range(nch):
                    sync.dma_start(out=l1x[128 * c:128 * (c + 1), 128 * i:128 * (i + 1)],
                                   in_=lhbufs[i % 2][:, 128 * c:128 * (c + 1)]).then_inc(st, 16)
                    inc("st")
                ev[f"lst_{i}"] = cnt["st"]
                nxt = i + 2
                if nxt < ntile:
                    sync.wait_ge(vs, ev[f"xn_{i}"])
                    sync.wait_ge(sc, ev[f"sq1_{i}"])
                    sync.dma_start(out=xbuf[j][:, :], in_=x1[128 * nxt:128 * (nxt + 1), :]).then_inc(ld[j], 16)
                    inc(f"ld{j}")

        @block.sync
        def _(sync):
            sync.wait_ge(vs, cnt["vs"])
            for c in range(nch):
                sync.dma_start(out=xn1f[128 * c:128 * (c + 1), :], in_=fmf[c][:, :]).then_inc(st, 16)
                inc("st")

        @block.gpsimd
        def _(gpsimd):
            gpsimd.wait_ge(vs, cnt["vs"])
            for c in range(nch):
                gpsimd.dma_start(out=xn1h[128 * c:128 * (c + 1), :], in_=fmf[c][:, :]).then_inc(ldg, 16)
                inc("ldg")
            gpsimd.wait_ge(ldg, 16 * cnt["ldg"])

        @block.sync
        def _(sync):
            sync.wait_ge(st, 16 * cnt["st"])

    return nc


def ln1_inputs(x1b, g1, b1):
    C = 1024
    return dict(
        x1=np.ascontiguousarray(x1b),
        g1bc=np.ascontiguousarray(np.broadcast_to(g1, (128, C))),
        b1bc=np.ascontiguousarray(np.broadcast_to(b1, (128, C))),
        idn=np.eye(128, dtype=BF),
        idnf=np.eye(128, dtype=np.float32),
    )


def build_attn1a(T=4096, C=1024):
    nc = bass.Bass("TRN2", target_bir_lowering=False, detect_race_conditions=False)
    nch = C // 128
    NTG = T // 512
    NG = T // 64
    l1x = nc.dram_tensor("l1x", [C, T], BF16, kind="ExternalInput").ap()
    xn1f = nc.dram_tensor("xn1f", [C, T], F32, kind="ExternalInput").ap()
    W_tm = nc.dram_tensor("W_tm", [C, 3 * C], F32, kind="ExternalInput").ap()
    tmkf = nc.dram_tensor("tmkf", [128, nch], F32, kind="ExternalInput").ap()
    tmvf = nc.dram_tensor("tmvf", [128, nch], F32, kind="ExternalInput").ap()
    tmrf = nc.dram_tensor("tmrf", [128, nch], F32, kind="ExternalInput").ap()
    wfm = nc.dram_tensor("wfm", [128, nch], F32, kind="ExternalInput").ap()
    jwpat = nc.dram_tensor("jwpat", [128, T], F32, kind="ExternalInput").ap()
    mask01 = nc.dram_tensor("mask01", [128, T], F32, kind="ExternalInput").ap()
    kfm = nc.dram_tensor("kfm", [C, T], F32, kind="ExternalOutput").ap()
    vfm = nc.dram_tensor("vfm", [C, T], F32, kind="ExternalOutput").ap()
    rfm = nc.dram_tensor("rfm", [C, T], F32, kind="ExternalOutput").ap()
    gamal = nc.dram_tensor("gamal", [128, 8 * NG], F32, kind="ExternalOutput").ap()
    alpal = nc.dram_tensor("alpal", [128, 8 * NG], F32, kind="ExternalOutput").ap()

    NCOL = 8 * (NG // 8)
    ctx = Ctx(nc)
    sl1x = [ctx.sb(f"sl1x{c}", [128, T], BF16) for c in range(nch)]
    swtm = [ctx.sb(f"swtm{c}", [128, 3 * C], BF16) for c in range(nch)]
    stmk = ctx.sb("stmk", [128, nch], F32)
    stmv = ctx.sb("stmv", [128, nch], F32)
    stmr = ctx.sb("stmr", [128, nch], F32)
    stmk1 = ctx.sb("stmk1", [128, nch], F32)
    stmv1 = ctx.sb("stmv1", [128, nch], F32)
    stmr1 = ctx.sb("stmr1", [128, nch], F32)
    swf = ctx.sb("swf", [128, nch], F32)
    sw63 = ctx.sb("sw63", [128, nch], F32)
    sw64 = ctx.sb("sw64", [128, nch], F32)
    swexp = ctx.sb("swexp", [128, nch], F32)
    sxp = [ctx.sb(f"sxp{j}", [128, 512], F32) for j in range(2)]
    kvout = [ctx.sb(f"kvout{j}", [128, 512], F32) for j in range(6)]
    tmpm = ctx.sb("tmpm", [128, 512], F32)
    TW = min(T, 1024)
    sjw = ctx.sb("sjw", [128, TW], F32)
    smk = ctx.sb("smk", [128, TW], F32)
    skv = [ctx.sb(f"skv{j}", [128, TW], F32) for j in range(2)]
    sE = ctx.sb("sE", [128, TW], F32)
    sU = ctx.sb("sU", [128, TW], F32)
    sP = ctx.sb("sP", [128, TW], F32)
    sQ = ctx.sb("sQ", [128, TW], F32)
    steall = ctx.sb("steall", [128, 8 * NG], F32)
    stuall = ctx.sb("stuall", [128, 8 * NG], F32)
    slnte = ctx.sb("slnte", [128, 8 * NG], F32)
    sxall = ctx.sb("sxall", [128, 8 * NG], F32)
    sewz = ctx.sb("sewz", [128, 8 * NG], F32)
    sln1 = ctx.sb("sln1", [128, TW], F32)
    sln2 = ctx.sb("sln2", [128, TW], F32)
    smki = ctx.sb("smki", [128, TW], F32)
    sgam = ctx.sb("sgam", [128, 8 * NG], F32)
    salp = ctx.sb("salp", [128, 8 * NG], F32)
    sa = ctx.sb("sa", [128, nch], F32)
    sb_ = ctx.sb("sb_", [128, nch], F32)
    sbeta = ctx.sb("sbeta", [128, nch], F32)
    stau = ctx.sb("stau", [128, nch], F32)
    sp_ = ctx.sb("sp_", [128, nch], F32)
    sd1 = ctx.sb("sd1", [128, nch], F32)
    sd2 = ctx.sb("sd2", [128, nch], F32)
    se1 = ctx.sb("se1", [128, nch], F32)
    se2 = ctx.sb("se2", [128, nch], F32)
    sD = ctx.sb("sD", [128, nch], F32)
    slnD = ctx.sb("slnD", [128, nch], F32)
    srD = ctx.sb("srD", [128, nch], F32)
    se3 = ctx.sb("se3", [128, nch], F32)
    st1_ = ctx.sb("st1_", [128, nch], F32)
    st2_ = ctx.sb("st2_", [128, nch], F32)
    pmm = [ctx.ps(f"pmm{j}", [128, 512], F32) for j in range(2)]

    ldc = ctx.sem("ldc")
    ldg = ctx.sem("ldg")
    ldx = [ctx.sem(f"ldx{j}") for j in range(2)]
    ldk = ctx.sem("ldk")
    gp = ctx.sem("gp")
    vs = ctx.sem("vs")
    sc = ctx.sem("sc")
    te = ctx.sem("te")
    st = ctx.sem("st")
    cnt = {"ldc": 0, "ldg": 0, "ldx0": 0, "ldx1": 0, "ldk": 0,
           "vs": 0, "sc": 0, "te": 0, "st": 0}
    ev = {}

    def inc(n, k=1):
        cnt[n] += k
        return cnt[n]

    with ctx.es, nc.Block() as block:

        @block.sync
        def _(sync):
            for ap_, t_ in ((tmkf, stmk), (tmvf, stmv), (tmrf, stmr), (wfm, swf)):
                sync.dma_start(out=t_[:, :], in_=ap_).then_inc(ldc, 16)
                inc("ldc")
            sync.dma_start(out=sjw[:, :], in_=jwpat[:, 0:TW]).then_inc(ldc, 16)
            inc("ldc")
            sync.dma_start(out=smk[:, :], in_=mask01[:, 0:TW]).then_inc(ldc, 16)
            inc("ldc")
            for c in range(nch):
                sync.dma_start(out=sl1x[c][:, :], in_=l1x[128 * c:128 * (c + 1), :]).then_inc(ldc, 16)
                inc("ldc")
            ev["consts"] = cnt["ldc"]
            for q in (0, 1):
                cc, tg = q // NTG if False else 0, 0

        @block.gpsimd
        def _(gpsimd):
            for c in range(nch):
                gpsimd.dma_start(out=swtm[c][:, :], in_=W_tm[128 * c:128 * (c + 1), :]).then_inc(ldg, 16)
                inc("ldg")
            ev["w"] = cnt["ldg"]

        @block.vector
        def _(vector):
            vector.wait_ge(ldc, 16 * 6)
            for src, dst in ((stmk, stmk1), (stmv, stmv1), (stmr, stmr1)):
                vector.tensor_scalar(dst[:, :], src[:, :], -1.0, 1.0, ALU.mult, ALU.add)
                vector.drain()
            vector.tensor_scalar_mul(sw63[:, :], swf[:, :], 63.0)
            vector.tensor_scalar_mul(sw64[:, :], swf[:, :], 64.0)
            vector.tensor_scalar(smki[:, :], smk[:, 0:TW], -1.0, 1.0, ALU.mult, ALU.add)
            vector.drain().then_inc(vs, 1)
            ev["mixc"] = inc("vs")

        @block.scalar
        def _(scalar):
            scalar.wait_ge(vs, ev["mixc"])
            scalar.activation(swexp[:, :], swf[:, :], AF.Exp)
            scalar.drain().then_inc(sc, 1)
            ev["wexp"] = inc("sc")

        qi = 0
        for tg in range(NTG):
            t0 = 512 * tg
            for cc in range(nch):
                xj = qi % 2

                @block.sync
                def _(sync, tg=tg, cc=cc, qi=qi, xj=xj, t0=t0):
                    if qi >= 2:
                        sync.wait_ge(vs, ev[f"mixv_{qi - 2}"])
                    if t0 == 0:
                        sync.dma_start(out=sxp[xj][:, 0:1],
                                       in_=xn1f[128 * cc:128 * (cc + 1), 0:1],
                                       allow_slow_non_contiguous=True).then_inc(ldx[xj], 16)
                        inc(f"ldx{xj}")
                        sync.dma_start(out=sxp[xj][:, 1:512],
                                       in_=xn1f[128 * cc:128 * (cc + 1), 0:511]).then_inc(ldx[xj], 16)
                        inc(f"ldx{xj}")
                    else:
                        sync.dma_start(out=sxp[xj][:, :],
                                       in_=xn1f[128 * cc:128 * (cc + 1), t0 - 1:t0 + 511]).then_inc(ldx[xj], 16)
                        inc(f"ldx{xj}")
                    ev[f"xp_{qi}"] = cnt[f"ldx{xj}"]

                for pi, (part, tmc, tm1) in enumerate(
                        ((0, stmk, stmk1), (1, stmv, stmv1), (2, stmr, stmr1))):
                    pj = (3 * qi + pi) % 2

                    @block.tensor
                    def _(tensor, tg=tg, cc=cc, qi=qi, pi=pi, part=part, pj=pj, t0=t0):
                        if qi == 0 and pi == 0:
                            tensor.wait_ge(ldg, 16 * ev["w"])
                            tensor.wait_ge(ldc, 16 * ev["consts"])
                        key = f"mixe_{3 * qi + pi - 2}"
                        if key in ev:
                            tensor.wait_ge(vs if (3 * qi + pi - 2) % 3 != 2 else sc, ev[key])
                        for cs in range(nch):
                            mm = tensor.matmul(pmm[pj][:, :],
                                               swtm[cs][:, C * part + 128 * cc:C * part + 128 * (cc + 1)],
                                               sl1x[cs][:, t0:t0 + 512],
                                               start=(cs == 0), stop=(cs == nch - 1))
                        mm.then_inc(te, 1)
                        ev[f"mixmm_{3 * qi + pi}"] = inc("te")

                    if pi < 2:
                        @block.vector
                        def _(vector, tg=tg, cc=cc, qi=qi, pi=pi, pj=pj, xj=xj,
                              tmc=tmc, tm1=tm1):
                            vector.wait_ge(te, ev[f"mixmm_{3 * qi + pi}"])
                            vector.wait_ge(ldx[xj], 16 * ev[f"xp_{qi}"])
                            ob = kvout[2 * pi + qi % 2]
                            if qi >= 2:
                                vector.wait_ge(st, 16 * ev[f"kst_{qi - 2}_{pi}"])
                            vector.tensor_scalar(tmpm[:, :], sxp[xj][:, :], tm1[:, cc:cc + 1],
                                                 None, ALU.mult)
                            vector.drain()
                            vector.scalar_tensor_tensor(ob[:, :], pmm[pj][:, :], tmc[:, cc:cc + 1],
                                                        tmpm[:, :], ALU.mult, ALU.add)
                            vector.drain().then_inc(vs, 1)
                            ev[f"mixe_{3 * qi + pi}"] = inc("vs")
                            if pi == 1:
                                ev[f"mixv_{qi}"] = ev[f"mixe_{3 * qi + pi}"]
                    else:
                        @block.vector
                        def _(vector, qi=qi, cc=cc, pi=pi, pj=pj, xj=xj, tmc=tmc, tm1=tm1):
                            vector.wait_ge(te, ev[f"mixmm_{3 * qi + pi}"])
                            vector.wait_ge(ldx[xj], 16 * ev[f"xp_{qi}"])
                            vector.tensor_scalar(tmpm[:, :], sxp[xj][:, :], tm1[:, cc:cc + 1],
                                                 None, ALU.mult)
                            vector.drain()
                            vector.scalar_tensor_tensor(tmpm[:, :], pmm[pj][:, :], tmc[:, cc:cc + 1],
                                                        tmpm[:, :], ALU.mult, ALU.add)
                            vector.drain().then_inc(vs, 1)
                            ev[f"rmixv_{qi}"] = inc("vs")

                        @block.scalar
                        def _(scalar, qi=qi):
                            scalar.wait_ge(vs, ev[f"rmixv_{qi}"])
                            if qi >= 2:
                                scalar.wait_ge(st, 16 * ev[f"kst_{qi - 2}_2"])
                            ob = kvout[4 + qi % 2]
                            scalar.activation(ob[:, :], tmpm[:, :], AF.Sigmoid)
                            scalar.drain().then_inc(sc, 1)
                            ev[f"mixe_{3 * qi + 2}"] = inc("sc")

                @block.sync
                def _(sync, tg=tg, cc=cc, qi=qi, t0=t0):
                    for pi, dst in ((0, kfm), (1, vfm), (2, rfm)):
                        if pi < 2:
                            sync.wait_ge(vs, ev[f"mixe_{3 * qi + pi}"])
                        else:
                            sync.wait_ge(sc, ev[f"mixe_{3 * qi + pi}"])
                        ob = kvout[2 * pi + qi % 2]
                        sync.dma_start(out=dst[128 * cc:128 * (cc + 1), t0:t0 + 512],
                                       in_=ob[:, :]).then_inc(st, 16)
                        inc("st")
                        ev[f"kst_{qi}_{pi}"] = cnt["st"]

                qi += 1

        NH = T // TW
        hi = 0
        for s in range(nch):
            for hh in range(NH):
                @block.sync
                def _(sync, s=s, hh=hh, hi=hi):
                    if hi == 0:
                        sync.wait_ge(st, 16 * cnt["st"])
                    else:
                        sync.wait_ge(vs, ev[f"tu_{hi - 1}"])
                    sync.dma_start(out=skv[0][:, :],
                                   in_=kfm[128 * s:128 * (s + 1), TW * hh:TW * (hh + 1)]).then_inc(ldk, 16)
                    inc("ldk")
                    ev[f"kvld_{hi}"] = cnt["ldk"]

                @block.vector
                def _(vector, s=s, hi=hi):
                    vector.wait_ge(ldk, 16 * ev[f"kvld_{hi}"])
                    vector.tensor_scalar(sE[:, :], sjw[:, :], swf[:, s:s + 1], None, ALU.mult)
                    vector.drain()
                    vector.tensor_sub(sE[:, :], skv[0][:, :], sE[:, :])
                    vector.drain().then_inc(vs, 1)
                    ev[f"d_{hi}"] = inc("vs")

                @block.scalar
                def _(scalar, hi=hi):
                    scalar.wait_ge(vs, ev[f"d_{hi}"])
                    scalar.activation(sE[:, :], sE[:, :], AF.Exp)
                    scalar.drain().then_inc(sc, 1)
                    ev[f"E_{hi}"] = inc("sc")

                @block.vector
                def _(vector, s=s, hh=hh, hi=hi):
                    vector.wait_ge(sc, ev[f"E_{hi}"])
                    vector.tensor_tensor_scan(sP[:, :], smk[:, :], sE[:, :], 0.0, ALU.mult, ALU.add)
                    vector.drain()
                    g0 = TW * hh // 64
                    ng = TW // 64
                    vector.tensor_copy(steall[:, 8 * g0 + s:8 * (g0 + ng - 1) + s + 1:8],
                                       sP[:, 63:TW:64])
                    vector.drain().then_inc(vs, 1)
                    ev[f"tu_{hi}"] = inc("vs")

                hi += 1
        ev[f"tu_{nch - 1}"] = ev[f"tu_{hi - 1}"]

        @block.vector
        def _(vector):
            vector.wait_ge(vs, ev[f"tu_{nch - 1}"])
            slnu = sQ[:, 0:8 * NG].bitcast(U32)
            slne = sP[:, 0:8 * NG]
            u = steall[:, :].bitcast(U32)
            vector.tensor_scalar(slnu, u, 23, None, ALU.logical_shift_right)
            vector.drain()
            vector.tensor_copy(slne, slnu)
            vector.drain()
            vector.tensor_scalar(slne, slne, -127.0, None, ALU.add)
            vector.drain()
            vector.tensor_scalar(slnu, u, 0x007FFFFF, None, ALU.bitwise_and)
            vector.drain()
            vector.tensor_scalar(slnu, slnu, 0x3F800000, None, ALU.bitwise_or)
            vector.drain().then_inc(vs, 1)
            ev["lnv1"] = inc("vs")

        @block.scalar
        def _(scalar):
            scalar.wait_ge(vs, ev["lnv1"])
            scalar.activation(slnte[:, :], sQ[:, 0:8 * NG], AF.Ln)
            scalar.drain().then_inc(sc, 1)
            ev["lnm"] = inc("sc")

        @block.vector
        def _(vector):
            vector.wait_ge(sc, ev["lnm"])
            slne = sP[:, 0:8 * NG]
            vector.scalar_tensor_tensor(slnte[:, :], slne, float(np.log(2.0)),
                                        slnte[:, :], ALU.mult, ALU.add)
            vector.drain().then_inc(vs, 1)
            ev["lnte2"] = inc("vs")

        @block.vector
        def _(vector):
            vector.wait_ge(vs, ev["lnte2"])
            vector.wait_ge(ldc, 16 * 6)
            vector.memset(sa[:, :], 0.0)
            vector.memset(sb_[:, :], -1e38)
            vector.memset(sgam[:, 0:8], 1.0)
            vector.memset(salp[:, :], 0.0)
            vector.drain().then_inc(vs, 1)
            ev["init"] = inc("vs")

        for g in range(NG):
            @block.vector
            def _(vector, g=g):
                if g == 0:
                    vector.wait_ge(vs, ev["init"])
                    vector.wait_ge(sc, ev["wexp"])
                else:
                    vector.wait_ge(sc, ev[f"sse_{g - 1}"])
                vector.tensor_add(sbeta[:, :], sb_[:, :], sw64[:, :])
                vector.tensor_add(stau[:, :], slnte[:, 8 * g:8 * (g + 1)], sw63[:, :])
                vector.drain()
                vector.tensor_max(sp_[:, :], sbeta[:, :], stau[:, :])
                vector.drain()
                vector.tensor_sub(sd1[:, :], sbeta[:, :], sp_[:, :])
                vector.tensor_sub(sd2[:, :], stau[:, :], sp_[:, :])
                vector.drain().then_inc(vs, 1)
                ev[f"ssv_{g}"] = inc("vs")

            @block.scalar
            def _(scalar, g=g):
                scalar.wait_ge(vs, ev[f"ssv_{g}"])
                scalar.activation(se1[:, :], sd1[:, :], AF.Exp)
                scalar.drain()
                scalar.activation(se2[:, :], sd2[:, :], AF.Exp)
                scalar.drain().then_inc(sc, 1)
                ev[f"sse1_{g}"] = inc("sc")

            @block.vector
            def _(vector, g=g):
                vector.wait_ge(sc, ev[f"sse1_{g}"])
                vector.tensor_add(sD[:, :], se1[:, :], se2[:, :])
                vector.drain().then_inc(vs, 1)
                ev[f"ssD_{g}"] = inc("vs")

            @block.scalar
            def _(scalar, g=g):
                scalar.wait_ge(vs, ev[f"ssD_{g}"])
                scalar.activation(slnD[:, :], sD[:, :], AF.Ln)
                scalar.drain().then_inc(sc, 1)
                ev[f"sse2_{g}"] = inc("sc")

            @block.vector
            def _(vector, g=g):
                vector.wait_ge(sc, ev[f"sse2_{g}"])
                vector.tensor_add(sb_[:, :], sp_[:, :], slnD[:, :])
                vector.drain().then_inc(vs, 1)
                ev[f"ssb_{g}"] = inc("vs")

            if g < NG - 1:
                @block.vector
                def _(vector, g=g):
                    vector.wait_ge(vs, ev[f"ssb_{g}"])
                    vector.tensor_scalar_mul(st2_[:, :], sb_[:, :], -1.0)
                    vector.drain().then_inc(vs, 1)
                    ev[f"ssg_{g}"] = inc("vs")

                @block.scalar
                def _(scalar, g=g):
                    scalar.wait_ge(vs, ev[f"ssg_{g}"])
                    scalar.activation(sgam[:, 8 * (g + 1):8 * (g + 2)], st2_[:, :], AF.Exp)
                    scalar.drain().then_inc(sc, 1)
                    ev[f"sse_{g}"] = inc("sc")
            else:
                @block.scalar
                def _(scalar, g=g):
                    scalar.wait_ge(vs, ev[f"ssb_{g}"])
                    scalar.drain().then_inc(sc, 1)
                    ev[f"sse_{g}"] = inc("sc")

        @block.vector
        def _(vector):
            vector.wait_ge(sc, ev[f"sse_{NG - 1}"])
            vector.memset(sewz[:, 0:8], 0.0)
            for g in range(1, NG):
                vector.tensor_copy(sewz[:, 8 * g:8 * (g + 1)], swexp[:, :])
            vector.drain().then_inc(vs, 1)
            ev["ewz"] = inc("vs")

        hi2 = 0
        for s in range(nch):
            for hh in range(NH):
                g0 = TW * hh // 64
                ng = TW // 64

                def gex(buf, s=s, g0=g0, ng=ng):
                    return bass.AP(buf, 8 * g0 + s, [[8 * NG, 128], [8, ng], [0, 64]])

                @block.sync
                def _(sync, s=s, hh=hh, hi2=hi2):
                    if hi2 == 0:
                        sync.wait_ge(vs, ev["ewz"])
                    else:
                        sync.wait_ge(vs, ev[f"wvx_{hi2 - 1}"])
                    sync.dma_start(out=skv[0][:, :],
                                   in_=kfm[128 * s:128 * (s + 1), TW * hh:TW * (hh + 1)]).then_inc(ldk, 16)
                    inc("ldk")
                    sync.dma_start(out=skv[1][:, :],
                                   in_=vfm[128 * s:128 * (s + 1), TW * hh:TW * (hh + 1)]).then_inc(ldk, 16)
                    inc("ldk")
                    ev[f"kvld2_{hi2}"] = cnt["ldk"]

                @block.vector
                def _(vector, s=s, hi2=hi2):
                    vector.wait_ge(ldk, 16 * ev[f"kvld2_{hi2}"])
                    vector.tensor_scalar(sE[:, :], sjw[:, :], swf[:, s:s + 1], None, ALU.mult)
                    vector.drain()
                    vector.tensor_sub(sE[:, :], skv[0][:, :], sE[:, :])
                    vector.drain().then_inc(vs, 1)
                    ev[f"d2_{hi2}"] = inc("vs")

                @block.scalar
                def _(scalar, hi2=hi2):
                    scalar.wait_ge(vs, ev[f"d2_{hi2}"])
                    scalar.activation(sE[:, :], sE[:, :], AF.Exp)
                    scalar.drain().then_inc(sc, 1)
                    ev[f"E2_{hi2}"] = inc("sc")

                @block.vector
                def _(vector, s=s, hh=hh, hi2=hi2, g0=g0, ng=ng, gex=gex):
                    vector.wait_ge(sc, ev[f"E2_{hi2}"])
                    vector.tensor_mul(sU[:, :], sE[:, :], skv[1][:, :])
                    vector.drain()
                    vector.tensor_mul(sE[:, :], sE[:, :], gex(sgam))
                    vector.drain()
                    vector.tensor_mul(sU[:, :], sU[:, :], gex(sgam))
                    vector.drain()
                    vector.tensor_tensor_scan(sP[:, :], smk[:, :], sE[:, :], 0.0, ALU.mult, ALU.add)
                    vector.drain()
                    vector.tensor_add(sP[:, :], sP[:, :], gex(sewz))
                    vector.drain()
                    vector.tensor_copy(sQ[:, 1:TW], sP[:, 0:TW - 1])
                    vector.drain()
                    vector.tensor_copy(sQ[:, 0:TW:64], sewz[:, 8 * g0 + s:8 * (g0 + ng - 1) + s + 1:8])
                    vector.drain()
                    vector.tensor_max(sQ[:, :], sQ[:, :], sE[:, :])
                    vector.drain().then_inc(vs, 1)
                    ev[f"m1_{hi2}"] = inc("vs")
                    vector.tensor_copy(sxall[:, 8 * g0 + s:8 * (g0 + ng - 1) + s + 1:8],
                                       sQ[:, 63:TW:64])
                    vector.drain()
                    um = sQ[:, :].bitcast(U32)
                    vector.tensor_scalar(sln1[:, :].bitcast(U32), um, 23, None, ALU.logical_shift_right)
                    vector.drain()
                    vector.tensor_copy(sln2[:, :], sln1[:, :].bitcast(U32))
                    vector.drain()
                    vector.tensor_scalar(sln2[:, :], sln2[:, :], -127.0, None, ALU.add)
                    vector.drain()
                    vector.tensor_scalar(sln1[:, :].bitcast(U32), um, 0x007FFFFF, None, ALU.bitwise_and)
                    vector.drain()
                    vector.tensor_scalar(sln1[:, :].bitcast(U32), sln1[:, :].bitcast(U32),
                                         0x3F800000, None, ALU.bitwise_or)
                    vector.drain().then_inc(vs, 1)
                    ev[f"sl1_{hi2}"] = inc("vs")

                @block.scalar
                def _(scalar, hi2=hi2):
                    scalar.wait_ge(vs, ev[f"sl1_{hi2}"])
                    scalar.activation(sln1[:, :], sln1[:, :], AF.Ln)
                    scalar.drain().then_inc(sc, 1)
                    ev[f"sl2_{hi2}"] = inc("sc")

                @block.vector
                def _(vector, hi2=hi2):
                    vector.wait_ge(sc, ev[f"sl2_{hi2}"])
                    vector.scalar_tensor_tensor(sln1[:, :], sln2[:, :], float(np.log(2.0)),
                                                sln1[:, :], ALU.mult, ALU.add)
                    vector.drain().then_inc(vs, 1)
                    ev[f"sl3_{hi2}"] = inc("vs")

                @block.scalar
                def _(scalar, hi2=hi2):
                    scalar.wait_ge(vs, ev[f"sl3_{hi2}"])
                    scalar.activation(sln1[:, :], sln1[:, :], AF.Exp, scale=-1.0)
                    scalar.drain().then_inc(sc, 1)
                    ev[f"rm_{hi2}"] = inc("sc")

                @block.vector
                def _(vector, s=s, hh=hh, hi2=hi2, g0=g0, ng=ng, gex=gex):
                    vector.wait_ge(sc, ev[f"rm_{hi2}"])
                    vector.tensor_mul(sln1[:, :], sP[:, :], sln1[:, :])
                    vector.drain()
                    vector.memset(sln2[:, 0:1], 0.0)
                    vector.tensor_copy(sln2[:, 1:TW], sln1[:, 0:TW - 1])
                    vector.drain()
                    vector.tensor_mul(sln2[:, :], sln2[:, :], smk[:, :])
                    vector.drain()
                    vector.tensor_tensor_scan(sln1[:, :], sln2[:, :], sU[:, :], 0.0, ALU.mult, ALU.add)
                    vector.drain()
                    vector.tensor_copy(stuall[:, 8 * g0 + s:8 * (g0 + ng - 1) + s + 1:8],
                                       sln1[:, 63:TW:64])
                    vector.drain()
                    vector.tensor_tensor_scan(sln1[:, :], sln2[:, :], smki[:, :], 0.0, ALU.mult, ALU.add)
                    vector.drain()
                    vector.tensor_copy(steall[:, 8 * g0 + s:8 * (g0 + ng - 1) + s + 1:8],
                                       sln1[:, 63:TW:64])
                    vector.drain().then_inc(vs, 1)
                    ev[f"wvx_{hi2}"] = inc("vs")

                hi2 += 1

        @block.vector
        def _(vector):
            vector.wait_ge(vs, ev[f"wvx_{hi2 - 1}"])
            u = sxall[:, :].bitcast(U32)
            slnu = sQ[:, 0:8 * NG].bitcast(U32)
            slne = sP[:, 0:8 * NG]
            vector.tensor_scalar(slnu, u, 23, None, ALU.logical_shift_right)
            vector.drain()
            vector.tensor_copy(slne, slnu)
            vector.drain()
            vector.tensor_scalar(slne, slne, -127.0, None, ALU.add)
            vector.drain()
            vector.tensor_scalar(slnu, u, 0x007FFFFF, None, ALU.bitwise_and)
            vector.drain()
            vector.tensor_scalar(slnu, slnu, 0x3F800000, None, ALU.bitwise_or)
            vector.drain().then_inc(vs, 1)
            ev["xl1"] = inc("vs")

        @block.scalar
        def _(scalar):
            scalar.wait_ge(vs, ev["xl1"])
            scalar.activation(sxall[:, :], sQ[:, 0:8 * NG], AF.Ln)
            scalar.drain().then_inc(sc, 1)
            ev["xl2"] = inc("sc")

        @block.vector
        def _(vector):
            vector.wait_ge(sc, ev["xl2"])
            slne = sP[:, 0:8 * NG]
            vector.scalar_tensor_tensor(sxall[:, :], slne, float(np.log(2.0)),
                                        sxall[:, :], ALU.mult, ALU.add)
            vector.drain().then_inc(vs, 1)
            ev["xl3"] = inc("vs")

        @block.scalar
        def _(scalar):
            scalar.wait_ge(vs, ev["xl3"])
            scalar.activation(sxall[:, :], sxall[:, :], AF.Exp, scale=-1.0)
            scalar.drain().then_inc(sc, 1)
            ev["xrecip"] = inc("sc")

        @block.vector
        def _(vector):
            vector.wait_ge(sc, ev["xrecip"])
            vector.tensor_mul(steall[:, :], steall[:, :], sxall[:, :])
            vector.drain()
            vector.tensor_mul(stuall[:, :], stuall[:, :], sxall[:, :])
            vector.drain().then_inc(vs, 1)
            ev["coef"] = inc("vs")

        @block.vector
        def _(vector):
            vector.wait_ge(vs, ev["coef"])
            vector.memset(sa[:, :], 0.0)
            vector.drain().then_inc(vs, 1)
            ev["a0"] = inc("vs")

        for g in range(NG):
            @block.vector
            def _(vector, g=g):
                if g == 0:
                    vector.wait_ge(vs, ev["a0"])
                if g > 0:
                    vector.tensor_mul(salp[:, 8 * g:8 * (g + 1)], sa[:, :], swexp[:, :])
                    vector.drain()
                    vector.tensor_mul(st1_[:, :], salp[:, 8 * g:8 * (g + 1)],
                                      steall[:, 8 * g:8 * (g + 1)])
                else:
                    vector.tensor_mul(st1_[:, :], sa[:, :], steall[:, 0:8])
                vector.drain()
                vector.tensor_add(sa[:, :], st1_[:, :], stuall[:, 8 * g:8 * (g + 1)])
                vector.drain().then_inc(vs, 1)
                ev[f"asc_{g}"] = inc("vs")

        @block.sync
        def _(sync):
            sync.wait_ge(vs, ev[f"asc_{NG - 1}"])
            sync.dma_start(out=gamal, in_=sgam[:, :]).then_inc(st, 16)
            inc("st")
            sync.dma_start(out=alpal, in_=salp[:, :]).then_inc(st, 16)
            inc("st")
            sync.wait_ge(st, 16 * cnt["st"])

    return nc


def build_attn1b(T=4096, C=1024):
    nc = bass.Bass("TRN2", target_bir_lowering=False, detect_race_conditions=False)
    nch = C // 128
    NG = T // 64
    TW = min(T, 2048)
    NH = T // TW
    ngh = TW // 64
    kfm = nc.dram_tensor("kfm", [C, T], F32, kind="ExternalInput").ap()
    vfm = nc.dram_tensor("vfm", [C, T], F32, kind="ExternalInput").ap()
    rfm = nc.dram_tensor("rfm", [C, T], F32, kind="ExternalInput").ap()
    xn1f = nc.dram_tensor("xn1f", [C, T], F32, kind="ExternalInput").ap()
    gamal = nc.dram_tensor("gamal", [128, 8 * NG], F32, kind="ExternalInput").ap()
    alpal = nc.dram_tensor("alpal", [128, 8 * NG], F32, kind="ExternalInput").ap()
    wfm = nc.dram_tensor("wfm", [128, nch], F32, kind="ExternalInput").ap()
    ufm = nc.dram_tensor("ufm", [128, nch], F32, kind="ExternalInput").ap()
    jwpat = nc.dram_tensor("jwpat", [128, T], F32, kind="ExternalInput").ap()
    mask01 = nc.dram_tensor("mask01", [128, T], F32, kind="ExternalInput").ap()
    xlocf = nc.dram_tensor("xlocf", [C, T], F32, kind="ExternalOutput").ap()

    ctx = Ctx(nc)
    sjw = ctx.sb("sjw", [128, TW], F32)
    smk = ctx.sb("smk", [128, TW], F32)
    sk = ctx.sb("sk", [128, TW], F32)
    sv = ctx.sb("sv", [128, TW], F32)
    sr = ctx.sb("sr", [128, TW], F32)
    sxn = ctx.sb("sxn", [128, TW], F32)
    sE = ctx.sb("sE", [128, TW], F32)
    sU = ctx.sb("sU", [128, TW], F32)
    sP = ctx.sb("sP", [128, TW], F32)
    sQ = ctx.sb("sQ", [128, TW], F32)
    sD1 = ctx.sb("sD1", [128, TW], F32)
    sN1 = ctx.sb("sN1", [128, TW], F32)
    st1 = ctx.sb("st1", [128, TW], F32)
    sden = ctx.sb("sden", [128, TW], F32)
    snum = ctx.sb("snum", [128, TW], F32)
    sy = ctx.sb("sy", [128, TW], F32)
    sdu = ctx.sb("sdu", [128, TW], U32)
    sde = ctx.sb("sde", [128, TW], F32)
    sgam = ctx.sb("sgam", [128, 8 * NG], F32)
    salp = ctx.sb("salp", [128, 8 * NG], F32)
    sewz = ctx.sb("sewz", [128, 8 * NG], F32)
    swf = ctx.sb("swf", [128, nch], F32)
    suf = ctx.sb("suf", [128, nch], F32)
    swexp = ctx.sb("swexp", [128, nch], F32)
    scu = ctx.sb("scu", [128, nch], F32)
    suw = ctx.sb("suw", [128, nch], F32)

    ldc = ctx.sem("ldc")
    ldk = [ctx.sem(f"ldk{j}") for j in range(2)]
    vs = ctx.sem("vs")
    sc = ctx.sem("sc")
    st = ctx.sem("st")
    cnt = {"ldc": 0, "ldk0": 0, "ldk1": 0, "vs": 0, "sc": 0, "st": 0}
    ev = {}

    def inc(n, k=1):
        cnt[n] += k
        return cnt[n]

    def gexp(buf, s, g0, ng):
        return bass.AP(buf, 8 * g0 + s, [[8 * NG, 128], [8, ng], [0, 64]])

    with ctx.es, nc.Block() as block:

        @block.sync
        def _(sync):
            for ap_, t_ in ((gamal, sgam), (alpal, salp), (wfm, swf), (ufm, suf)):
                sync.dma_start(out=t_[:, :], in_=ap_).then_inc(ldc, 16)
                inc("ldc")
            sync.dma_start(out=sjw[:, :], in_=jwpat[:, 0:TW]).then_inc(ldc, 16)
            inc("ldc")
            sync.dma_start(out=smk[:, :], in_=mask01[:, 0:TW]).then_inc(ldc, 16)
            inc("ldc")
            ev["c"] = cnt["ldc"]

        @block.vector
        def _(vector):
            vector.wait_ge(ldc, 16 * ev["c"])
            vector.tensor_add(suw[:, :], suf[:, :], swf[:, :])
            vector.drain().then_inc(vs, 1)
            ev["uw"] = inc("vs")

        @block.scalar
        def _(scalar):
            scalar.wait_ge(vs, ev["uw"])
            scalar.activation(scu[:, :], suw[:, :], AF.Exp)
            scalar.drain()
            scalar.activation(swexp[:, :], swf[:, :], AF.Exp)
            scalar.drain().then_inc(sc, 1)
            ev["cu0"] = inc("sc")

        @block.vector
        def _(vector):
            vector.wait_ge(sc, ev["cu0"])
            vector.tensor_scalar_add(scu[:, :], scu[:, :], -1.0)
            vector.drain()
            vector.memset(sewz[:, 0:8], 0.0)
            for g in range(1, NG):
                vector.tensor_copy(sewz[:, 8 * g:8 * (g + 1)], swexp[:, :])
            vector.drain().then_inc(vs, 1)
            ev["cu"] = inc("vs")

        hi = 0
        for s in range(nch):
            for hh in range(NH):
                jj = hi % 2
                c0 = TW * hh
                g0 = c0 // 64

                @block.sync
                def _(sync, s=s, hh=hh, hi=hi, jj=jj, c0=c0):
                    if hi >= 2:
                        sync.wait_ge(vs, ev[f"oh_{hi - 2}"])
                    for src, dst in ((kfm, sk), (vfm, sv), (rfm, sr), (xn1f, sxn)):
                        sync.dma_start(out=dst[:, :],
                                       in_=src[128 * s:128 * (s + 1), c0:c0 + TW]).then_inc(ldk[jj], 16)
                        inc(f"ldk{jj}")
                    ev[f"ld_{hi}"] = cnt[f"ldk{jj}"]

                @block.vector
                def _(vector, s=s, hi=hi, jj=jj):
                    vector.wait_ge(ldk[jj], 16 * ev[f"ld_{hi}"])
                    vector.wait_ge(vs, ev["cu"])
                    vector.tensor_scalar(sE[:, :], sjw[:, :], swf[:, s:s + 1], None, ALU.mult)
                    vector.drain()
                    vector.tensor_sub(sE[:, :], sk[:, :], sE[:, :])
                    vector.drain().then_inc(vs, 1)
                    ev[f"d_{hi}"] = inc("vs")

                @block.scalar
                def _(scalar, hi=hi):
                    scalar.wait_ge(vs, ev[f"d_{hi}"])
                    scalar.activation(sE[:, :], sE[:, :], AF.Exp)
                    scalar.drain().then_inc(sc, 1)
                    ev[f"E_{hi}"] = inc("sc")

                @block.vector
                def _(vector, s=s, hh=hh, hi=hi, g0=g0):
                    vector.wait_ge(sc, ev[f"E_{hi}"])
                    ng = ngh
                    vector.tensor_mul(sU[:, :], sE[:, :], sv[:, :])
                    vector.drain()
                    vector.tensor_mul(sE[:, :], sE[:, :], gexp(sgam, s, g0, ng))
                    vector.drain()
                    vector.tensor_mul(sU[:, :], sU[:, :], gexp(sgam, s, g0, ng))
                    vector.drain()
                    vector.tensor_tensor_scan(sP[:, :], smk[:, :], sE[:, :], 0.0, ALU.mult, ALU.add)
                    vector.drain()
                    vector.tensor_add(sP[:, :], sP[:, :], gexp(sewz, s, g0, ng))
                    vector.drain()
                    vector.tensor_copy(sQ[:, 1:TW], sP[:, 0:TW - 1])
                    vector.drain()
                    vector.tensor_copy(sQ[:, 0:TW:64], sewz[:, 8 * g0 + s:8 * (g0 + TW // 64 - 1) + s + 1:8])
                    vector.drain()
                    vector.tensor_max(sQ[:, :], sQ[:, :], sE[:, :])
                    vector.drain().then_inc(vs, 1)
                    ev[f"m1_{hi}"] = inc("vs")
                    um = sQ[:, :].bitcast(U32)
                    vector.tensor_scalar(sD1[:, :].bitcast(U32), um, 23, None, ALU.logical_shift_right)
                    vector.drain()
                    vector.tensor_copy(sN1[:, :], sD1[:, :].bitcast(U32))
                    vector.drain()
                    vector.tensor_scalar(sN1[:, :], sN1[:, :], -127.0, None, ALU.add)
                    vector.drain()
                    vector.tensor_scalar(sD1[:, :].bitcast(U32), um, 0x007FFFFF, None, ALU.bitwise_and)
                    vector.drain()
                    vector.tensor_scalar(sD1[:, :].bitcast(U32), sD1[:, :].bitcast(U32),
                                         0x3F800000, None, ALU.bitwise_or)
                    vector.drain().then_inc(vs, 1)
                    ev[f"ml1_{hi}"] = inc("vs")

                @block.scalar
                def _(scalar, hi=hi):
                    scalar.wait_ge(vs, ev[f"ml1_{hi}"])
                    scalar.activation(sD1[:, :], sD1[:, :], AF.Ln)
                    scalar.drain().then_inc(sc, 1)
                    ev[f"ml2_{hi}"] = inc("sc")

                @block.vector
                def _(vector, hi=hi):
                    vector.wait_ge(sc, ev[f"ml2_{hi}"])
                    vector.scalar_tensor_tensor(sD1[:, :], sN1[:, :], float(np.log(2.0)),
                                                sD1[:, :], ALU.mult, ALU.add)
                    vector.drain().then_inc(vs, 1)
                    ev[f"ml3_{hi}"] = inc("vs")

                @block.scalar
                def _(scalar, hi=hi):
                    scalar.wait_ge(vs, ev[f"ml3_{hi}"])
                    scalar.activation(sD1[:, :], sD1[:, :], AF.Exp, scale=-1.0)
                    scalar.drain().then_inc(sc, 1)
                    ev[f"mrm_{hi}"] = inc("sc")

                @block.vector
                def _(vector, s=s, hh=hh, hi=hi, g0=g0):
                    vector.wait_ge(sc, ev[f"mrm_{hi}"])
                    ng = TW // 64
                    vector.tensor_mul(sD1[:, :], sP[:, :], sD1[:, :])
                    vector.drain()
                    vector.memset(sN1[:, 0:1], 0.0)
                    vector.tensor_copy(sN1[:, 1:TW], sD1[:, 0:TW - 1])
                    vector.drain()
                    vector.tensor_mul(sN1[:, :], sN1[:, :], smk[:, :])
                    vector.drain()
                    vector.tensor_add(sU[:, 0:TW:64], sU[:, 0:TW:64],
                                      salp[:, 8 * g0 + s:8 * (g0 + ng - 1) + s + 1:8])
                    vector.drain()
                    vector.tensor_tensor_scan(st1[:, :], sN1[:, :], sU[:, :], 0.0, ALU.mult, ALU.add)
                    vector.drain()
                    vector.tensor_sub(sU[:, 0:TW:64], sU[:, 0:TW:64],
                                      salp[:, 8 * g0 + s:8 * (g0 + ng - 1) + s + 1:8])
                    vector.drain()
                    vector.scalar_tensor_tensor(snum[:, :], sU[:, :], scu[:, s:s + 1], st1[:, :],
                                                ALU.mult, ALU.add)
                    vector.drain()
                    vector.scalar_tensor_tensor(sden[:, :], sE[:, :], scu[:, s:s + 1], sP[:, :],
                                                ALU.mult, ALU.add)
                    vector.drain().then_inc(vs, 1)
                    ev[f"nd_{hi}"] = inc("vs")

                @block.vector
                def _(vector, hi=hi):
                    vector.wait_ge(vs, ev[f"nd_{hi}"])
                    u = sden[:, :].bitcast(U32)
                    vector.tensor_scalar(sdu[:, :], u, 23, None, ALU.logical_shift_right)
                    vector.drain()
                    vector.tensor_copy(sde[:, :], sdu[:, :])
                    vector.drain()
                    vector.tensor_scalar(sde[:, :], sde[:, :], -127.0, None, ALU.add)
                    vector.drain()
                    vector.tensor_scalar(sdu[:, :], u, 0x007FFFFF, None, ALU.bitwise_and)
                    vector.drain()
                    vector.tensor_scalar(sdu[:, :], sdu[:, :], 0x3F800000, None, ALU.bitwise_or)
                    vector.drain().then_inc(vs, 1)
                    ev[f"sl1_{hi}"] = inc("vs")

                @block.scalar
                def _(scalar, hi=hi):
                    scalar.wait_ge(vs, ev[f"sl1_{hi}"])
                    scalar.activation(st1[:, :], sdu[:, :].bitcast(F32), AF.Ln)
                    scalar.drain().then_inc(sc, 1)
                    ev[f"sl2_{hi}"] = inc("sc")

                @block.vector
                def _(vector, hi=hi):
                    vector.wait_ge(sc, ev[f"sl2_{hi}"])
                    vector.scalar_tensor_tensor(sden[:, :], sde[:, :], float(np.log(2.0)),
                                                st1[:, :], ALU.mult, ALU.add)
                    vector.drain().then_inc(vs, 1)
                    ev[f"sl3_{hi}"] = inc("vs")

                @block.scalar
                def _(scalar, hi=hi):
                    scalar.wait_ge(vs, ev[f"sl3_{hi}"])
                    scalar.activation(sden[:, :], sden[:, :], AF.Exp, scale=-1.0)
                    scalar.drain().then_inc(sc, 1)
                    ev[f"rd_{hi}"] = inc("sc")

                @block.vector
                def _(vector, hi=hi):
                    vector.wait_ge(sc, ev[f"rd_{hi}"])
                    vector.tensor_mul(sy[:, :], snum[:, :], sden[:, :])
                    vector.drain()
                    vector.tensor_mul(sy[:, :], sy[:, :], sr[:, :])
                    vector.drain()
                    vector.tensor_add(sy[:, :], sy[:, :], sxn[:, :])
                    vector.drain().then_inc(vs, 1)
                    ev[f"oh_{hi}"] = inc("vs")

                @block.sync
                def _(sync, s=s, hi=hi, c0=c0):
                    sync.wait_ge(vs, ev[f"oh_{hi}"])
                    sync.dma_start(out=xlocf[128 * s:128 * (s + 1), c0:c0 + TW],
                                   in_=sy[:, :]).then_inc(st, 16)
                    inc("st")

                hi += 1

        @block.sync
        def _(sync):
            sync.wait_ge(st, 16 * cnt["st"])

    return nc




def build_attn1c(T=4096, C=1024):
    nc = bass.Bass("TRN2", target_bir_lowering=False, detect_race_conditions=False)
    nch = C // 128
    C4 = 4 * C
    nc4 = C4 // 128
    GW = 256
    NGR = T // GW
    xlocf = nc.dram_tensor("xlocf", [C, T], F32, kind="ExternalInput").ap()
    W_cm = nc.dram_tensor("W_cm", [C, C4], F32, kind="ExternalInput").ap()
    W_cp = nc.dram_tensor("W_cp", [C4, C], F32, kind="ExternalInput").ap()
    cmkf = nc.dram_tensor("cmkf", [128, nch], F32, kind="ExternalInput").ap()
    cmrf = nc.dram_tensor("cmrf", [128, nch], F32, kind="ExternalInput").ap()
    g2f = nc.dram_tensor("g2f", [128, nch], F32, kind="ExternalInput").ap()
    b2f = nc.dram_tensor("b2f", [128, nch], F32, kind="ExternalInput").ap()
    ones128 = nc.dram_tensor("ones128", [128, 128], F32, kind="ExternalInput").ap()
    idnf = nc.dram_tensor("idnf", [128, 128], F32, kind="ExternalInput").ap()
    attn1tm = nc.dram_tensor("attn1tm", [T, C], F32, kind="ExternalOutput").ap()

    ctx = Ctx(nc)
    swcm = [ctx.sb(f"swcm{c}", [128, C4], BF16) for c in range(nch)]
    swcp = [ctx.sb(f"swcp{m}", [128, C], BF16) for m in range(nc4)]
    scmk = ctx.sb("scmk", [128, nch], F32)
    scmk1 = ctx.sb("scmk1", [128, nch], F32)
    scmr = ctx.sb("scmr", [128, nch], F32)
    scmr1 = ctx.sb("scmr1", [128, nch], F32)
    sg2 = ctx.sb("sg2", [128, nch], F32)
    sb2c = ctx.sb("sb2c", [128, nch], F32)
    sone = ctx.sb("sone", [128, 128], F32)
    sidnf = ctx.sb("sidnf", [128, 128], F32)
    sxl = [[ctx.sb(f"sxl{j}_{s}", [128, GW + 1], F32) for s in range(nch)] for j in range(2)]
    sxn2 = [ctx.sb(f"sxn2_{s}", [128, GW], F32) for s in range(nch)]
    sxn2h = [ctx.sb(f"sxn2h_{s}", [128, GW], BF16) for s in range(nch)]
    sxp1 = ctx.sb("sxp1", [128, GW], F32)
    ssq = [ctx.sb(f"ssq{m}", [128, GW], BF16) for m in range(nc4)]
    srr = ctx.sb("srr", [128, GW], F32)
    smbc = ctx.sb("smbc", [128, GW], F32)
    svbc = ctx.sb("svbc", [128, GW], F32)
    srstd = ctx.sb("srstd", [128, GW], F32)
    stmp = ctx.sb("stmp", [128, GW], F32)
    souts = [ctx.sb(f"souts{j}", [128, GW], F32) for j in range(2)]
    stm = [ctx.sb(f"stm{j}", [128, C], F32) for j in range(2)]
    epscol = ctx.sb("epscol", [128, 1], F32)
    pstat = [ctx.ps(f"pstat{j}", [128, GW], F32) for j in range(2)]
    pmm = [ctx.ps(f"pmm{j}", [128, GW], F32) for j in range(2)]
    ptr = [ctx.ps(f"ptr{j}", [128, GW], F32) for j in range(2)]

    ldc = ctx.sem("ldc")
    ldg = ctx.sem("ldg")
    ldx = [ctx.sem(f"ldx{j}") for j in range(2)]
    gp = ctx.sem("gp")
    vs = ctx.sem("vs")
    sc = ctx.sem("sc")
    te = ctx.sem("te")
    st = [ctx.sem(f"st{j}") for j in range(2)]
    cnt = {"ldc": 0, "ldg": 0, "ldx0": 0, "ldx1": 0, "vs": 0, "sc": 0, "te": 0,
           "st0": 0, "st1": 0}
    ev = {}

    def inc(n, k=1):
        cnt[n] += k
        return cnt[n]

    with ctx.es, nc.Block() as block:

        @block.sync
        def _(sync):
            for ap_, t_ in ((cmkf, scmk), (cmrf, scmr), (g2f, sg2), (b2f, sb2c),
                            (ones128, sone), (idnf, sidnf)):
                sync.dma_start(out=t_[:, :], in_=ap_).then_inc(ldc, 16)
                inc("ldc")
            ev["c"] = cnt["ldc"]

        @block.gpsimd
        def _(gpsimd):
            gpsimd.memset(epscol[:, :], float(C * LN_EPS)).then_inc(gp, 1)
            for c in range(nch):
                gpsimd.dma_start(out=swcm[c][:, :], in_=W_cm[128 * c:128 * (c + 1), :]).then_inc(ldg, 16)
                inc("ldg")
            for m in range(nc4):
                gpsimd.dma_start(out=swcp[m][:, :], in_=W_cp[128 * m:128 * (m + 1), :]).then_inc(ldg, 16)
                inc("ldg")
            ev["w"] = cnt["ldg"]

        @block.vector
        def _(vector):
            vector.wait_ge(ldc, 16 * ev["c"])
            vector.tensor_scalar(scmk1[:, :], scmk[:, :], -1.0, 1.0, ALU.mult, ALU.add)
            vector.drain()
            vector.tensor_scalar(scmr1[:, :], scmr[:, :], -1.0, 1.0, ALU.mult, ALU.add)
            vector.drain().then_inc(vs, 1)
            ev["mixc"] = inc("vs")

        for gr in range(NGR):
            jj = gr % 2
            t0 = GW * gr

            @block.sync
            def _(sync, gr=gr, jj=jj, t0=t0):
                if gr >= 2:
                    sync.wait_ge(vs, ev[f"grdone_{gr - 2}"])
                for s in range(nch):
                    if gr == 0:
                        sync.dma_start(out=sxl[jj][s][:, 0:1],
                                       in_=xlocf[128 * s:128 * (s + 1), T - 1:T],
                                       allow_slow_non_contiguous=True).then_inc(ldx[jj], 16)
                        inc(f"ldx{jj}")
                        sync.dma_start(out=sxl[jj][s][:, 1:GW + 1],
                                       in_=xlocf[128 * s:128 * (s + 1), 0:GW]).then_inc(ldx[jj], 16)
                        inc(f"ldx{jj}")
                    else:
                        sync.dma_start(out=sxl[jj][s][:, :],
                                       in_=xlocf[128 * s:128 * (s + 1), t0 - 1:t0 + GW]).then_inc(ldx[jj], 16)
                        inc(f"ldx{jj}")
                ev[f"xld_{gr}"] = cnt[f"ldx{jj}"]

            @block.tensor
            def _(tensor, gr=gr, jj=jj):
                tensor.wait_ge(ldx[jj], 16 * ev[f"xld_{gr}"])
                tensor.wait_ge(ldc, 16 * ev["c"])
                if gr >= 1:
                    tensor.wait_ge(vs, ev[f"stats_used_{gr - 1}"])
                for s in range(nch):
                    mm = tensor.matmul(pstat[0][:, :], sone[:, :], sxl[jj][s][:, 1:GW + 1],
                                       start=(s == 0), stop=(s == nch - 1))
                mm.then_inc(te, 1)
                ev[f"sum_{gr}"] = inc("te")

            @block.scalar
            def _(scalar, gr=gr, jj=jj):
                scalar.wait_ge(ldx[jj], 16 * ev[f"xld_{gr}"])
                if gr >= 1:
                    scalar.wait_ge(te, ev[f"sq2mm_{gr - 1}"])
                for s in range(nch):
                    scalar.activation(sxn2[s][:, :], sxl[jj][s][:, 1:GW + 1], AF.Square)
                scalar.drain().then_inc(sc, 1)
                ev[f"sqs_{gr}"] = inc("sc")

            @block.tensor
            def _(tensor, gr=gr):
                tensor.wait_ge(sc, ev[f"sqs_{gr}"])
                if gr >= 1:
                    tensor.wait_ge(vs, ev[f"stats_used_{gr - 1}"])
                for s in range(nch):
                    mm = tensor.matmul(pstat[1][:, :], sone[:, :], sxn2[s][:, :],
                                       start=(s == 0), stop=(s == nch - 1))
                mm.then_inc(te, 1)
                ev[f"sq2mm_{gr}"] = inc("te")

            @block.vector
            def _(vector, gr=gr):
                vector.wait_ge(te, ev[f"sq2mm_{gr}"])
                vector.wait_ge(te, ev[f"sum_{gr}"])
                vector.tensor_scalar(smbc[:, :], pstat[0][:, :], 1.0 / C, None, ALU.mult)
                vector.drain()
                vector.tensor_mul(stmp[:, :], smbc[:, :], smbc[:, :])
                vector.tensor_scalar(svbc[:, :], pstat[1][:, :], 1.0 / C, None, ALU.mult)
                vector.drain()
                vector.tensor_sub(svbc[:, :], svbc[:, :], stmp[:, :])
                vector.drain().then_inc(vs, 1)
                ev[f"var_{gr}"] = inc("vs")
                ev[f"stats_used_{gr}"] = ev[f"var_{gr}"]

            @block.scalar
            def _(scalar, gr=gr):
                scalar.wait_ge(vs, ev[f"var_{gr}"])
                scalar.wait_ge(gp, 1)
                scalar.activation(srstd[:, :], svbc[:, :], AF.Ln, bias=epscol[:, :], scale=float(C))
                scalar.drain()
                scalar.activation(srstd[:, :], srstd[:, :], AF.Exp, scale=-0.5)
                scalar.drain().then_inc(sc, 1)
                ev[f"rstd_{gr}"] = inc("sc")

            @block.vector
            def _(vector, gr=gr, jj=jj):
                vector.wait_ge(sc, ev[f"rstd_{gr}"])
                if gr >= 1:
                    vector.wait_ge(te, ev[f"kkmm_{gr - 1}_{nc4 - 1}"])
                vector.tensor_scalar(srstd[:, :], srstd[:, :], float(np.sqrt(C)), None, ALU.mult)
                vector.drain()
                for s in range(nch):
                    vector.tensor_sub(sxn2[s][:, :], sxl[jj][s][:, 1:GW + 1], smbc[:, :])
                    vector.drain()
                    vector.tensor_mul(sxn2[s][:, :], sxn2[s][:, :], srstd[:, :])
                    vector.drain()
                    vector.tensor_scalar(sxn2[s][:, :], sxn2[s][:, :], sg2[:, s:s + 1],
                                         sb2c[:, s:s + 1], ALU.mult, ALU.add)
                    vector.drain()
                    vector.tensor_copy(sxn2h[s][:, :], sxn2[s][:, :])
                    vector.drain()
                vector.drain().then_inc(vs, 1)
                ev[f"xn2_{gr}"] = inc("vs")

            for m in range(nc4):
                pj = m % 2

                @block.tensor
                def _(tensor, gr=gr, m=m, pj=pj):
                    tensor.wait_ge(vs, ev[f"xn2_{gr}"])
                    if m == 0 and gr == 0:
                        tensor.wait_ge(ldg, 16 * ev["w"])
                    key = f"sqe_{gr}_{m - 2}"
                    if key in ev:
                        tensor.wait_ge(sc, ev[key])
                    elif gr >= 1:
                        tensor.wait_ge(sc, ev[f"sqe_{gr - 1}_{nc4 - 2 + m}"])
                    for s in range(nch):
                        mm = tensor.matmul(pmm[pj][:, :], swcm[s][:, 128 * m:128 * (m + 1)],
                                           sxn2h[s][:, :], start=(s == 0), stop=(s == nch - 1))
                    mm.then_inc(te, 1)
                    ev[f"kkmm_{gr}_{m}"] = inc("te")

                @block.vector
                def _(vector, gr=gr, jj=jj, m=m, pj=pj):
                    vector.wait_ge(te, ev[f"kkmm_{gr}_{m}"])
                    if m >= 1:
                        vector.wait_ge(sc, ev[f"sqe_{gr}_{m - 1}"])
                    s8 = m % nch
                    vector.tensor_scalar(stmp[:, :], sxl[jj][s8][:, 0:GW], scmk1[:, s8:s8 + 1],
                                         None, ALU.mult)
                    vector.drain()
                    vector.scalar_tensor_tensor(stmp[:, :], pmm[pj][:, :], scmk[:, s8:s8 + 1],
                                                stmp[:, :], ALU.mult, ALU.add)
                    vector.drain()
                    vector.tensor_scalar_max(stmp[:, :], stmp[:, :], 0.0)
                    vector.drain().then_inc(vs, 1)
                    ev[f"kkm_{gr}_{m}"] = inc("vs")

                @block.scalar
                def _(scalar, gr=gr, m=m):
                    scalar.wait_ge(vs, ev[f"kkm_{gr}_{m}"])
                    if m == 0 and gr >= 1:
                        scalar.wait_ge(te, ev[f"vvmm_{gr - 1}_{nch - 1}"])
                    scalar.activation(ssq[m][:, :], stmp[:, :], AF.Square)
                    scalar.drain().then_inc(sc, 1)
                    ev[f"sqe_{gr}_{m}"] = inc("sc")

            for s in range(nch):
                pj = s % 2

                @block.vector
                def _(vector, gr=gr, jj=jj, s=s):
                    if s == 0:
                        vector.wait_ge(sc, ev[f"sqe_{gr}_{nc4 - 1}"])
                    vector.tensor_scalar(srr[:, :], sxl[jj][s][:, 0:GW], scmr1[:, s:s + 1],
                                         None, ALU.mult)
                    vector.drain()
                    vector.scalar_tensor_tensor(srr[:, :], sxn2[s][:, :], scmr[:, s:s + 1],
                                                srr[:, :], ALU.mult, ALU.add)
                    vector.drain().then_inc(vs, 1)
                    ev[f"rrv_{gr}_{s}"] = inc("vs")

                @block.scalar
                def _(scalar, gr=gr, s=s):
                    scalar.wait_ge(vs, ev[f"rrv_{gr}_{s}"])
                    scalar.activation(srr[:, :], srr[:, :], AF.Sigmoid)
                    scalar.drain().then_inc(sc, 1)
                    ev[f"rr_{gr}_{s}"] = inc("sc")

                @block.tensor
                def _(tensor, gr=gr, s=s, pj=pj):
                    if s == 0:
                        tensor.wait_ge(sc, ev[f"sqe_{gr}_{nc4 - 1}"])
                    key = f"oute_{gr}_{s - 2}"
                    if key in ev:
                        tensor.wait_ge(vs, ev[key])
                    elif gr >= 1:
                        tensor.wait_ge(vs, ev[f"oute_{gr - 1}_{nch - 2 + s}"])
                    for m in range(nc4):
                        mm = tensor.matmul(pmm[pj][:, :], swcp[m][:, 128 * s:128 * (s + 1)],
                                           ssq[m][:, :], start=(m == 0), stop=(m == nc4 - 1))
                    mm.then_inc(te, 1)
                    ev[f"vvmm_{gr}_{s}"] = inc("te")

                @block.vector
                def _(vector, gr=gr, jj=jj, s=s, pj=pj):
                    vector.wait_ge(te, ev[f"vvmm_{gr}_{s}"])
                    vector.wait_ge(sc, ev[f"rr_{gr}_{s}"])
                    ob = souts[s % 2]
                    if s >= 2:
                        vector.wait_ge(te, ev[f"trv_{gr}_{s - 2}"])
                    vector.tensor_mul(ob[:, :], srr[:, :], pmm[pj][:, :])
                    vector.drain()
                    vector.tensor_add(ob[:, :], ob[:, :], sxl[jj][s][:, 1:GW + 1])
                    vector.drain().then_inc(vs, 1)
                    ev[f"oute_{gr}_{s}"] = inc("vs")

                @block.tensor
                def _(tensor, gr=gr, s=s):
                    tensor.wait_ge(vs, ev[f"oute_{gr}_{s}"])
                    if s == 0 and gr >= 1:
                        tensor.wait_ge(vs, ev[f"tme_{gr - 1}_{GW // 128 - 1}"])
                    ob = souts[s % 2]
                    for q in range(GW // 128):
                        mm = tensor.transpose(ptr[q % 2][:, 0:128] if False else ptr[q][:, 0:128],
                                              ob[:, 128 * q:128 * (q + 1)], sidnf[:, :])
                    mm.then_inc(te, 1)
                    ev[f"trv_{gr}_{s}"] = inc("te")

                @block.vector
                def _(vector, gr=gr, s=s):
                    vector.wait_ge(te, ev[f"trv_{gr}_{s}"])
                    if s == 0 and gr >= 1:
                        for q in range(GW // 128):
                            vector.wait_ge(st[q % 2], 16 * ev[f"tmst_{gr - 1}_{q}"])
                    for q in range(GW // 128):
                        vector.tensor_copy(stm[q][:, 128 * s:128 * (s + 1)], ptr[q][:, 0:128])
                    vector.drain().then_inc(vs, 1)
                    ev[f"tme_{gr}_{s}"] = inc("vs")

            @block.sync
            def _(sync, gr=gr, t0=t0):
                for q in range(GW // 128):
                    sync.wait_ge(vs, ev[f"tme_{gr}_{nch - 1}"])
                    jq = q % 2
                    if gr >= 2:
                        sync.wait_ge(st[jq], 16 * ev[f"tmst_{gr - 2}_{q}"])
                    sync.dma_start(out=attn1tm[t0 + 128 * q:t0 + 128 * (q + 1), :],
                                   in_=stm[q][:, :]).then_inc(st[jq], 16)
                    inc(f"st{jq}")
                    ev[f"tmst_{gr}_{q}"] = cnt[f"st{jq}"]
            ev[f"grdone_{gr}"] = ev[f"tme_{gr}_{nch - 1}"]

        @block.sync
        def _(sync):
            sync.wait_ge(st[0], 16 * cnt["st0"])
            sync.wait_ge(st[1], 16 * cnt["st1"])

    return nc


def build_ffn1(SL=512, C=1024, NB=8):
    nc = bass.Bass("TRN2", target_bir_lowering=False, detect_race_conditions=False)
    nch = C // 128
    xng = nc.dram_tensor("xng", [C, NB * SL], BF16, kind="ExternalInput").ap()
    mats = {}
    for nm in ("wfT", "wiT", "wcT", "woT", "ufT", "uiT", "ucT", "uoT"):
        mats[nm] = nc.dram_tensor(nm, [C, C], F32, kind="ExternalInput").ap()
    bff = nc.dram_tensor("bff", [128, nch], F32, kind="ExternalInput").ap()
    bif = nc.dram_tensor("bif", [128, nch], F32, kind="ExternalInput").ap()
    bcf = nc.dram_tensor("bcf", [128, nch], F32, kind="ExternalInput").ap()
    bof = nc.dram_tensor("bof", [128, nch], F32, kind="ExternalInput").ap()
    incf = nc.dram_tensor("incf", [128, nch], F32, kind="ExternalInput").ap()
    idnf = nc.dram_tensor("idnf", [128, 128], F32, kind="ExternalInput").ap()
    Fb = nc.dram_tensor("Fb", [NB, SL, C], F32, kind="ExternalOutput").ap()
    cdbg = nc.dram_tensor("cdbg", [128, 8 * SL], F32, kind="ExternalOutput").ap()

    ctx = Ctx(nc)
    sw = {nm: [ctx.sb(f"s{nm}{c}", [128, C], BF16) for c in range(nch)] for nm in mats}
    sx = [[ctx.sb(f"sx{j}_{c}", [128, SL], BF16) for c in range(nch)] for j in range(2)]
    scst = [ctx.sb(f"scst{m}", [128, SL], F32) for m in range(nch)]
    scbf = [ctx.sb(f"scbf{m}", [128, SL], BF16) for m in range(nch)]
    sf = ctx.sb("sf", [128, SL], F32)
    si_ = ctx.sb("si_", [128, SL], F32)
    sg_ = ctx.sb("sg_", [128, SL], F32)
    sh = ctx.sb("sh", [128, SL], F32)
    stmo4 = [ctx.sb(f"stmo4_{q}", [128, C], F32) for q in range(4)]
    sbf = ctx.sb("sbf", [128, nch], F32)
    sbi = ctx.sb("sbi", [128, nch], F32)
    sbc = ctx.sb("sbc", [128, nch], F32)
    sbo = ctx.sb("sbo", [128, nch], F32)
    sinc = ctx.sb("sinc", [128, nch], F32)
    sidnf = ctx.sb("sidnf", [128, 128], F32)
    pg = [ctx.ps(f"pg{j}", [128, SL], F32) for j in range(2)]
    ptr = [ctx.ps(f"ptr{j}", [128, 128], F32) for j in range(SL // 128)]

    ldc = ctx.sem("ldc")
    ldg = ctx.sem("ldg")
    ldx = [ctx.sem(f"ldx{j}") for j in range(2)]
    vs = ctx.sem("vs")
    sc = ctx.sem("sc")
    te = ctx.sem("te")
    st = ctx.sem("st")
    cnt = {"ldc": 0, "ldg": 0, "ldx0": 0, "ldx1": 0, "vs": 0, "sc": 0, "te": 0, "st": 0}
    ev = {}

    def inc(n, k=1):
        cnt[n] += k
        return cnt[n]

    with ctx.es, nc.Block() as block:

        @block.sync
        def _(sync):
            for ap_, t_ in ((bff, sbf), (bif, sbi), (bcf, sbc), (bof, sbo),
                            (incf, sinc), (idnf, sidnf)):
                sync.dma_start(out=t_[:, :], in_=ap_).then_inc(ldc, 16)
                inc("ldc")
            ev["c"] = cnt["ldc"]
            for b in (0, 1):
                for c in range(nch):
                    sync.dma_start(out=sx[b][c][:, :],
                                   in_=xng[128 * c:128 * (c + 1), SL * b:SL * (b + 1)]).then_inc(ldx[b], 16)
                    inc(f"ldx{b}")
                ev[f"xld_{b}"] = cnt[f"ldx{b}"]

        @block.gpsimd
        def _(gpsimd):
            for nm in ("wfT", "wiT", "wcT", "ufT", "uiT", "ucT", "woT", "uoT"):
                for c in range(nch):
                    gpsimd.dma_start(out=sw[nm][c][:, :],
                                     in_=mats[nm][128 * c:128 * (c + 1), :]).then_inc(ldg, 16)
                    inc("ldg")
            ev["w"] = cnt["ldg"]

        @block.vector
        def _(vector):
            vector.wait_ge(ldc, 16 * ev["c"])
            for m in range(nch):
                vector.memset(scst[m][:, :], 0.0)
                vector.drain()
                vector.tensor_scalar(scst[m][:, :], scst[m][:, :], sinc[:, m:m + 1], None, ALU.add)
                vector.drain()
                vector.tensor_copy(scbf[m][:, :], scst[m][:, :])
                vector.drain()
            vector.drain().then_inc(vs, 1)
            ev["cinit"] = inc("vs")

        for b in range(NB):
            xj = b % 2

            if b >= 2:
                @block.sync
                def _(sync, b=b, xj=xj):
                    q_last = 3 * ((b - 2) * nch + nch - 1) + 2
                    sync.wait_ge(te, ev[f"gmm_{q_last}"])
                    for c in range(nch):
                        sync.dma_start(out=sx[xj][c][:, :],
                                       in_=xng[128 * c:128 * (c + 1), SL * b:SL * (b + 1)]).then_inc(ldx[xj], 16)
                        inc(f"ldx{xj}")
                    ev[f"xld_{b}"] = cnt[f"ldx{xj}"]

            for m in range(nch):
                for gi, (wn, un, bb, ob, fn) in enumerate((
                        ("wfT", "ufT", sbf, sf, AF.Sigmoid),
                        ("wiT", "uiT", sbi, si_, AF.Sigmoid),
                        ("wcT", "ucT", sbc, sg_, AF.Tanh))):
                    pj = (3 * m + gi) % 2

                    @block.tensor
                    def _(tensor, b=b, m=m, gi=gi, wn=wn, un=un, pj=pj, xj=xj):
                        if b == 0 and m == 0 and gi == 0:
                            tensor.wait_ge(ldg, 16 * ev["w"])
                            tensor.wait_ge(vs, ev["cinit"])
                        if b >= 1 and m == 0 and gi == 0:
                            tensor.wait_ge(vs, ev[f"cref_{b - 1}"])
                        tensor.wait_ge(ldx[xj], 16 * ev[f"xld_{b}"])
                        q = 3 * (b * nch + m) + gi
                        key = f"gevac_{q - 2}"
                        if key in ev:
                            tensor.wait_ge(sc, ev[key])
                        for cs in range(nch):
                            tensor.matmul(pg[pj][:, :], sw[wn][cs][:, 128 * m:128 * (m + 1)],
                                          sx[xj][cs][:, :], start=(cs == 0), stop=False)
                        for cs in range(nch):
                            mm = tensor.matmul(pg[pj][:, :], sw[un][cs][:, 128 * m:128 * (m + 1)],
                                               scbf[cs][:, :], start=False, stop=(cs == nch - 1))
                        mm.then_inc(te, 1)
                        ev[f"gmm_{q}"] = inc("te")

                    @block.scalar
                    def _(scalar, b=b, m=m, gi=gi, bb=bb, ob=ob, fn=fn, pj=pj):
                        q = 3 * (b * nch + m) + gi
                        scalar.wait_ge(te, ev[f"gmm_{q}"])
                        if gi == 0 and (b > 0 or m > 0):
                            scalar.wait_ge(vs, ev[f"cupd_{b * nch + m - 1 if m > 0 else (b - 1) * nch + nch - 1}"])
                        scalar.activation(ob[:, :], pg[pj][:, :], fn, bias=bb[:, m:m + 1])
                        scalar.drain().then_inc(sc, 1)
                        ev[f"gevac_{q}"] = inc("sc")

                @block.vector
                def _(vector, b=b, m=m):
                    q = 3 * (b * nch + m)
                    vector.wait_ge(sc, ev[f"gevac_{q + 2}"])
                    vector.tensor_mul(scst[m][:, :], sf[:, :], scst[m][:, :])
                    vector.tensor_mul(sg_[:, :], si_[:, :], sg_[:, :])
                    vector.drain()
                    vector.tensor_add(scst[m][:, :], scst[m][:, :], sg_[:, :])
                    vector.drain().then_inc(vs, 1)
                    ev[f"cupd_{b * nch + m}"] = inc("vs")

            @block.vector
            def _(vector, b=b):
                vector.wait_ge(vs, ev[f"cupd_{b * nch + nch - 1}"])
                q_last = 3 * (b * nch + nch - 1) + 2
                vector.wait_ge(te, ev[f"gmm_{q_last}"])
                for m in range(nch):
                    vector.tensor_copy(scbf[m][:, :], scst[m][:, :])
                vector.drain().then_inc(vs, 1)
                ev[f"cref_{b}"] = inc("vs")

        @block.sync
        def _(sync):
            sync.wait_ge(vs, ev[f"cref_{NB - 1}"])
            for m in range(nch):
                sync.dma_start(out=cdbg[:, SL * m:SL * (m + 1)], in_=scst[m][:, :]).then_inc(st, 16)
                inc("st")
        for b in range(NB):
            xj = b % 2

            @block.sync
            def _(sync, b=b, xj=xj):
                if b >= 2:
                    sync.wait_ge(te, ev[f"hmm_{b - 2}_{nch - 1}"])
                for c in range(nch):
                    sync.dma_start(out=sx[xj][c][:, :],
                                   in_=xng[128 * c:128 * (c + 1), SL * b:SL * (b + 1)]).then_inc(ldx[xj], 16)
                    inc(f"ldx{xj}")
                ev[f"xld2_{b}"] = cnt[f"ldx{xj}"]

            for m in range(nch):
                pj = m % 2

                @block.tensor
                def _(tensor, b=b, m=m, pj=pj, xj=xj):
                    tensor.wait_ge(ldx[xj], 16 * ev[f"xld2_{b}"])
                    if b == 0 and m == 0:
                        tensor.wait_ge(vs, ev[f"cref_{NB - 1}"])
                    q = b * nch + m
                    key = f"hevac_{q - 2}"
                    if key in ev:
                        tensor.wait_ge(sc, ev[key])
                    for cs in range(nch):
                        tensor.matmul(pg[pj][:, :], sw["woT"][cs][:, 128 * m:128 * (m + 1)],
                                      sx[xj][cs][:, :], start=(cs == 0), stop=False)
                    for cs in range(nch):
                        mm = tensor.matmul(pg[pj][:, :], sw["uoT"][cs][:, 128 * m:128 * (m + 1)],
                                           scbf[cs][:, :], start=False, stop=(cs == nch - 1))
                    mm.then_inc(te, 1)
                    ev[f"hmm_{b}_{m}"] = inc("te")

                @block.scalar
                def _(scalar, b=b, m=m, pj=pj):
                    q = b * nch + m
                    scalar.wait_ge(te, ev[f"hmm_{b}_{m}"])
                    if q >= 1:
                        scalar.wait_ge(te, ev[f"trh_{q - 1}"])
                    scalar.activation(sh[:, :], pg[pj][:, :], AF.Tanh, bias=sbo[:, m:m + 1])
                    scalar.drain().then_inc(sc, 1)
                    ev[f"hevac_{q}"] = inc("sc")

                @block.tensor
                def _(tensor, b=b, m=m):
                    q = b * nch + m
                    tensor.wait_ge(sc, ev[f"hevac_{q}"])
                    tensor.wait_ge(ldc, 16 * ev["c"])
                    if q >= 1:
                        tensor.wait_ge(vs, ev[f"tre_{q - 1}"])
                    for qq in range(SL // 128):
                        mm = tensor.transpose(ptr[qq][:, :], sh[:, 128 * qq:128 * (qq + 1)],
                                              sidnf[:, :])
                    mm.then_inc(te, 1)
                    ev[f"trh_{q}"] = inc("te")

                @block.vector
                def _(vector, b=b, m=m):
                    q = b * nch + m
                    vector.wait_ge(te, ev[f"trh_{q}"])
                    if m == 0 and b >= 1:
                        vector.wait_ge(st, 16 * ev[f"fst_{b - 1}"])
                    for qq in range(SL // 128):
                        vector.tensor_copy(stmo4[qq][:, 128 * m:128 * (m + 1)], ptr[qq][:, :])
                    vector.drain().then_inc(vs, 1)
                    ev[f"tre_{q}"] = inc("vs")

            @block.sync
            def _(sync, b=b):
                sync.wait_ge(vs, ev[f"tre_{b * nch + nch - 1}"])
                for qq in range(SL // 128):
                    sync.dma_start(out=Fb[b, 128 * qq:128 * (qq + 1), :],
                                   in_=stmo4[qq][:, :]).then_inc(st, 16)
                    inc("st")
                ev[f"fst_{b}"] = cnt["st"]

        @block.sync
        def _(sync):
            sync.wait_ge(st, 16 * cnt["st"])

    return nc


def ffn1_inputs(xng, z):
    C = 1024
    def fmcol(v):
        return np.ascontiguousarray(v.reshape(-1, 128).T)
    d = dict(xng=xng,
             wfT=np.ascontiguousarray(z["L1_ffn_wf"].T), ufT=np.ascontiguousarray(z["L1_ffn_uf"].T),
             wiT=np.ascontiguousarray(z["L1_ffn_wi"].T), uiT=np.ascontiguousarray(z["L1_ffn_ui"].T),
             wcT=np.ascontiguousarray(z["L1_ffn_wc"].T), ucT=np.ascontiguousarray(z["L1_ffn_uc"].T),
             woT=np.ascontiguousarray(z["L1_ffn_wo"].T), uoT=np.ascontiguousarray(z["L1_ffn_uo"].T),
             bff=fmcol(z["L1_ffn_bf"]), bif=fmcol(z["L1_ffn_bi"]),
             bcf=fmcol(z["L1_ffn_bc"]), bof=fmcol(z["L1_ffn_bo"]),
             incf=fmcol(z["L1_ffn_in_c"]),
             idnf=np.eye(128, dtype=np.float32))
    return d


def build_final(T=4096, C=1024):
    nc = bass.Bass("TRN2", target_bir_lowering=False, detect_race_conditions=False)
    acc1 = nc.dram_tensor("acc1", [T, C], F32, kind="ExternalInput").ap()
    a1 = nc.dram_tensor("a1", [T, C], F32, kind="ExternalInput").ap()
    fb = nc.dram_tensor("fb", [T, C], F32, kind="ExternalInput").ap()
    outp = nc.dram_tensor("outp", [T, C], F32, kind="ExternalOutput").ap()
    ctx = Ctx(nc)
    b1 = [ctx.sb(f"b1_{j}", [128, C], F32) for j in range(2)]
    b2 = [ctx.sb(f"b2_{j}", [128, C], F32) for j in range(2)]
    b3 = [ctx.sb(f"b3_{j}", [128, C], F32) for j in range(2)]
    ob = [ctx.sb(f"ob_{j}", [128, C], F32) for j in range(2)]
    ld = [ctx.sem(f"ld{j}") for j in range(2)]
    vs = ctx.sem("vs")
    st = [ctx.sem(f"st{j}") for j in range(2)]
    ntile = T // 128
    ev = {}
    cnt = {"ld0": 0, "ld1": 0, "vs": 0, "st0": 0, "st1": 0}

    def inc(n, k=1):
        cnt[n] += k
        return cnt[n]

    with ctx.es, nc.Block() as block:
        for i in range(ntile):
            j = i % 2

            @block.sync
            def _(sync, i=i, j=j):
                if i >= 2:
                    sync.wait_ge(vs, ev[f"add_{i - 2}"])
                for src, dst in ((acc1, b1[j]), (a1, b2[j]), (fb, b3[j])):
                    sync.dma_start(out=dst[:, :], in_=src[128 * i:128 * (i + 1), :]).then_inc(ld[j], 16)
                    inc(f"ld{j}")
                ev[f"ld_{i}"] = cnt[f"ld{j}"]

            @block.vector
            def _(vector, i=i, j=j):
                vector.wait_ge(ld[j], 16 * ev[f"ld_{i}"])
                if i >= 2:
                    vector.wait_ge(st[j], 16 * ev[f"st_{i - 2}"])
                vector.tensor_add(ob[j][:, :], b1[j][:, :], b2[j][:, :])
                vector.drain()
                vector.tensor_add(ob[j][:, :], ob[j][:, :], b3[j][:, :])
                vector.drain().then_inc(vs, 1)
                ev[f"add_{i}"] = inc("vs")

            @block.sync
            def _(sync, i=i, j=j):
                sync.wait_ge(vs, ev[f"add_{i}"])
                sync.dma_start(out=outp[128 * i:128 * (i + 1), :], in_=ob[j][:, :]).then_inc(st[j], 16)
                inc(f"st{j}")
                ev[f"st_{i}"] = cnt[f"st{j}"]

        @block.sync
        def _(sync):
            sync.wait_ge(st[0], 16 * cnt["st0"])
            sync.wait_ge(st[1], 16 * cnt["st1"])

    return nc


from concourse.bass_utils import run_bass_kernel_spmd as _spmd


def _fmcol(v):
    return np.ascontiguousarray(np.asarray(v).reshape(-1, 128).T.astype(np.float32))


def kernel(**z):
    T, C = 4096, 1024
    cores = list(range(8))
    x0 = np.asarray(z["L0_attn_x"])

    nc = build_pre(T=T)
    im = [pre_inputs(x0[b], z["L0_ffn_gate_w"], z["L0_ffn_gate_b"]) for b in range(8)]
    r1 = _spmd(nc, im, core_ids=cores).results

    nc = build_attn0(T=T)
    im = [attn0_inputs(np.asarray(r1[b]["xn0fm"]), x0[b], z["L0_attn_W_qkv"],
                       z["L0_attn_b_qkv"], z["L0_attn_ln_w"], z["L0_attn_ln_b"])
          for b in range(8)]
    r2 = _spmd(nc, im, core_ids=cores).results

    nc = build_moe(T=T)
    im = [moe_inputs(np.asarray(r1[b]["xn0fm"]), np.asarray(r2[b]["acc0"]),
                     z["L0_ffn_w1"], z["L0_ffn_w2"], z["L0_ffn_b1"], z["L0_ffn_b2"],
                     np.asarray(r1[b]["gout"])) for b in range(8)]
    r3 = _spmd(nc, im, core_ids=cores).results

    nc = build_ln1(T=T)
    im = [ln1_inputs(np.asarray(r3[b]["acc1"]), z["L1_attn_g1"], z["L1_attn_b1"])
          for b in range(8)]
    r4 = _spmd(nc, im, core_ids=cores).results

    jwpat = np.ascontiguousarray(
        np.tile(np.arange(64, dtype=np.float32), T // 64)[None, :].repeat(128, 0))
    mask01 = np.ascontiguousarray(
        (np.arange(T) % 64 != 0).astype(np.float32)[None, :].repeat(128, 0))
    nc = build_attn1a(T=T)
    im = [dict(l1x=np.asarray(r4[b]["l1x"]), xn1f=np.asarray(r4[b]["xn1f"]),
               W_tm=np.ascontiguousarray(z["L1_attn_W_tm"]),
               tmkf=_fmcol(z["L1_attn_tmk"].ravel()), tmvf=_fmcol(z["L1_attn_tmv"].ravel()),
               tmrf=_fmcol(z["L1_attn_tmr"].ravel()),
               wfm=_fmcol(z["L1_attn_time_decay"]),
               jwpat=jwpat, mask01=mask01) for b in range(8)]
    r5 = _spmd(nc, im, core_ids=cores).results

    nc = build_attn1b(T=T)
    im = [dict(kfm=np.asarray(r5[b]["kfm"]), vfm=np.asarray(r5[b]["vfm"]),
               rfm=np.asarray(r5[b]["rfm"]), xn1f=np.asarray(r4[b]["xn1f"]),
               gamal=np.asarray(r5[b]["gamal"]), alpal=np.asarray(r5[b]["alpal"]),
               wfm=_fmcol(z["L1_attn_time_decay"]), ufm=_fmcol(z["L1_attn_time_first"]),
               jwpat=jwpat, mask01=mask01) for b in range(8)]
    r6 = _spmd(nc, im, core_ids=cores).results

    nc = build_attn1c(T=T)
    im = [dict(xlocf=np.asarray(r6[b]["xlocf"]),
               W_cm=np.ascontiguousarray(z["L1_attn_W_cm"]),
               W_cp=np.ascontiguousarray(z["L1_attn_W_cp"]),
               cmkf=_fmcol(z["L1_attn_cmk"].ravel()), cmrf=_fmcol(z["L1_attn_cmr"].ravel()),
               g2f=_fmcol(z["L1_attn_g2"]), b2f=_fmcol(z["L1_attn_b2"]),
               ones128=np.ones((128, 128), dtype=np.float32),
               idnf=np.eye(128, dtype=np.float32)) for b in range(8)]
    r7 = _spmd(nc, im, core_ids=cores).results

    xn1h = np.stack([np.asarray(r4[b]["xn1h"]) for b in range(8)])
    nc = build_ffn1(SL=512)
    im = []
    for i in range(8):
        sl = xn1h[:, :, 512 * i:512 * (i + 1)]
        xng = np.ascontiguousarray(sl.transpose(1, 0, 2).reshape(C, 8 * 512))
        im.append(ffn1_inputs(xng, z))
    r8 = _spmd(nc, im, core_ids=cores).results
    F = np.zeros((8, T, C), dtype=np.float32)
    for i in range(8):
        F[:, 512 * i:512 * (i + 1), :] = np.asarray(r8[i]["Fb"])

    nc = build_final(T=T)
    im = [dict(acc1=np.asarray(r3[b]["acc1"]), a1=np.asarray(r7[b]["attn1tm"]),
               fb=np.ascontiguousarray(F[b])) for b in range(8)]
    r9 = _spmd(nc, im, core_ids=cores).results
    return np.stack([np.asarray(r9[b]["outp"]) for b in range(8)])
